# version 08
# baseline (speedup 1.0000x reference)
import numpy as np
import ml_dtypes
from contextlib import ExitStack, contextmanager

import concourse.bass as bass
import concourse.mybir as mybir
from concourse.bass_utils import run_bass_kernel_spmd
from concourse.aot_env import allow_ici_all_to_all

F32 = mybir.dt.float32
BF16 = mybir.dt.bfloat16
FP8 = mybir.dt.float8e4
AF = mybir.ActivationFunctionType
ALU = mybir.AluOpType
AX = mybir.AxisListType

NCORE = 8
B, N, D = 2, 4096, 2048
NTOK = 1024
DC = 16
H = 16
DH = 128
EPS_LN = 1e-5
EPS_ATTN = 1e-6
GATE_NORM = 16.0


class Ctx:

    def __init__(self, nc, es):
        self.nc = nc
        self.es = es
        self.cnt = {}
        self.t = nc.tensor
        self.v = nc.vector
        self.s = nc.scalar
        self.g = nc.gpsimd
        self.y = nc.sync
        self._scopes = [es]

    def sem(self, name):
        h = self._scopes[0].enter_context(self.nc.semaphore(name))
        self.cnt[id(h)] = 0
        return h

    def inc(self, instr, sem, n=1):
        instr.then_inc(sem, n)
        self.cnt[id(sem)] += n
        return self.cnt[id(sem)]

    def val(self, sem):
        return self.cnt[id(sem)]

    _uid = 0

    def _un(self, name):
        Ctx._uid += 1
        return f"{name}_{Ctx._uid}"

    def sb(self, name, shape, dtype):
        return self._scopes[-1].enter_context(
            self.nc.sbuf_tensor(self._un(name), shape, dtype))

    def ps(self, name, shape=(128, 512), dtype=F32):
        return self._scopes[-1].enter_context(
            self.nc.psum_tensor(self._un(name), list(shape), dtype))

    @contextmanager
    def scope(self):
        es = ExitStack()
        self._scopes.append(es)
        try:
            yield
        finally:
            es.close()
            self._scopes.pop()


def emit_prepass(c, ap, nc):
    g = c.g
    se = c.sem("prepass")
    pre = {}
    pre["wgup"] = nc.dram_tensor("wgup_i", [64, 2, 16, 128, 128], BF16).ap()
    for i in range(8):
        c.inc(g.dma_start(out=pre["wgup"][i * 8:(i + 1) * 8],
                          in_=ap["wgup"][i * 8:(i + 1) * 8]), se, 16)
    pre["wdp"] = nc.dram_tensor("wdp_i", [64, 128, D], BF16).ap()
    for i in range(4):
        c.inc(g.dma_start(out=pre["wdp"][i * 16:(i + 1) * 16],
                          in_=ap["wdp"][i * 16:(i + 1) * 16]), se, 16)
    pre["wo"] = nc.dram_tensor("wo_i", [2, 8, 128, D], BF16).ap()
    c.inc(g.dma_start(out=pre["wo"], in_=ap["wo"]), se, 16)
    pre["w1t"] = nc.dram_tensor("w1t_i", [D, D], BF16).ap()
    c.inc(g.dma_start(out=pre["w1t"], in_=ap["w1t"]), se, 16)
    pre["d1t"] = nc.dram_tensor("d1t_i", [D, D], BF16).ap()
    c.inc(g.dma_start(out=pre["d1t"], in_=ap["d1t"]), se, 16)
    return pre, se, c.val(se)


def make_sems(c, tag):
    return {k: c.sem(tag + k) for k in
            ["cp", "sq", "mm", "ev", "dma", "wdma", "cc", "aux"]}


def join_all(c, sem):
    for e in (c.t, c.v, c.s, c.g, c.y):
        for sname in ("dma", "wdma", "mm", "ev", "cp", "sq", "aux", "cc"):
            if sname in sem:
                e.wait_ge(sem[sname], c.val(sem[sname]))



def emit_ln_fm(c, h, cst, ln_banks, sem, ntok=NTOK, hb_out=None):
    t, v, s = c.t, c.v, c.s
    se_cp, se_sq, se_mm, se_ev = sem["cp"], sem["sq"], sem["mm"], sem["ev"]
    ntb = ntok // 512
    bcast_bank = ln_banks[0]
    with c.scope():
        hb = c.sb("ln_hb", [128, 2 * ntok], BF16)
        sq = c.sb("ln_sq", [128, 2 * ntok], BF16)
        rb = c.sb("ln_rb", [128, ntok], F32)
        nbt = c.sb("ln_nb", [128, ntok], F32)
        row = c.sb("ln_row", [1, 4 * ntok], F32)

        mm_per_kc = 1
        for kc in range(DC):
            pp = kc % 2
            hk = h[:, kc * ntok:(kc + 1) * ntok]
            hbv = hb[:, pp * ntok:(pp + 1) * ntok]
            sqv = sq[:, pp * ntok:(pp + 1) * ntok]
            if kc >= 2:
                v.wait_ge(se_mm, c.val(se_mm) - mm_per_kc)
            c.inc(v.tensor_copy(hbv, hk), se_cp)
            if kc >= 2:
                s.wait_ge(se_mm, c.val(se_mm) - mm_per_kc)
            s.wait_ge(se_cp, c.val(se_cp))
            c.inc(s.activation(sqv, hbv, AF.Square), se_sq)
            t.wait_ge(se_sq, c.val(se_sq))
            mm = None
            for tb in range(ntb):
                t.matmul(ln_banks[tb][0:1, 0:512],
                         cst["onescol"], hbv[:, tb * 512:(tb + 1) * 512],
                         start=(kc == 0), stop=(kc == DC - 1))
                mm = t.matmul(ln_banks[2 + tb][0:1, 0:512],
                              cst["onescol"], sqv[:, tb * 512:(tb + 1) * 512],
                              start=(kc == 0), stop=(kc == DC - 1))
            c.inc(mm, se_mm)

        mrow = row[0:1, 0:ntok]
        vrow = row[0:1, ntok:2 * ntok]
        rrow = row[0:1, 2 * ntok:3 * ntok]
        nrow = row[0:1, 3 * ntok:4 * ntok]
        v.wait_ge(se_mm, c.val(se_mm))
        for tb in range(ntb):
            v.tensor_scalar(mrow[:, tb * 512:(tb + 1) * 512],
                            ln_banks[tb][0:1, 0:512], 1.0 / D, None, ALU.mult)
            v.tensor_scalar(vrow[:, tb * 512:(tb + 1) * 512],
                            ln_banks[2 + tb][0:1, 0:512], 1.0 / D, None,
                            ALU.mult)
        v.tensor_tensor(rrow, mrow, mrow, ALU.mult)
        c.inc(v.tensor_tensor(vrow, vrow, rrow, ALU.subtract), se_ev)
        s.wait_ge(se_ev, c.val(se_ev))
        s.activation(rrow, vrow, AF.Ln, bias=cst["eps_ln_1"])
        c.inc(s.activation(rrow, rrow, AF.Exp, scale=-0.5), se_sq)
        v.wait_ge(se_sq, c.val(se_sq))
        v.tensor_tensor(nrow, mrow, rrow, ALU.mult)
        c.inc(v.tensor_scalar(nrow, nrow, -1.0, None, ALU.mult), se_ev)

        t.wait_ge(se_ev, c.val(se_ev))
        for tb in range(ntb):
            for src, dst in ((rrow, rb), (nrow, nbt)):
                c.inc(t.matmul(bcast_bank[:, 0:512], cst["onesrow_f32"],
                               src[:, tb * 512:(tb + 1) * 512],
                               start=True, stop=True), se_mm)
                v.wait_ge(se_mm, c.val(se_mm))
                c.inc(v.tensor_copy(dst[:, tb * 512:(tb + 1) * 512],
                                    bcast_bank[:, 0:512]), se_ev)
                t.wait_ge(se_ev, c.val(se_ev))

        for kc in range(DC):
            hk = h[:, kc * ntok:(kc + 1) * ntok]
            v.tensor_tensor(hk, hk, rb[:, 0:ntok], ALU.mult)
            c.inc(v.tensor_tensor(hk, hk, nbt[:, 0:ntok], ALU.add), se_ev)
            if hb_out is not None:
                c.inc(v.tensor_copy(hb_out[:, kc * ntok:(kc + 1) * ntok], hk),
                      se_cp)



def emit_proj_fm(c, get_lhsT, get_rhs, evac, banks, sem, n_ms=16, ntb=2,
                 n_kc=16):
    t = c.t
    se_mm, se_ev = sem["mm"], sem["ev"]
    nb = len(banks)
    job = 0
    for ms in range(n_ms):
        for tb in range(ntb):
            bank = banks[job % nb]
            if job >= nb:
                t.wait_ge(se_ev, c.val(se_ev) - (nb - 1))
            i = None
            for kc in range(n_kc):
                i = t.matmul(bank[:, 0:512], get_lhsT(kc, ms), get_rhs(kc, tb),
                             start=(kc == 0), stop=(kc == n_kc - 1))
            c.inc(i, se_mm)
            evac(ms, tb, bank)
            job += 1
    t.wait_ge(se_ev, c.val(se_ev))


def partition_absmax(c, src_reducer, n_cols, sem, name, dst):
    nc = c.nc
    v, y = c.v, c.y
    se_dma, se_aux = sem["dma"], sem["aux"]
    with c.scope():
        amax = c.sb(name + "_amax", [128, n_cols], F32)
        row = c.sb(name + "_row", [1, 128 * n_cols], F32)
        bounce = nc.dram_tensor(name + "_bounce", [128, n_cols], F32).ap()
        src_reducer(amax)
        y.wait_ge(se_aux, c.val(se_aux))
        c.inc(y.dma_start(out=bounce, in_=amax[:]), se_dma, 16)
        y.wait_ge(se_dma, c.val(se_dma))
        c.inc(y.dma_start(
            out=row[:], in_=bounce.rearrange("p a -> (p a)")[None, :]),
            se_dma, 16)
        v.wait_ge(se_dma, c.val(se_dma))
        c.inc(v.tensor_reduce(dst, row[:], AX.X, ALU.max), se_aux)
        v.wait_ge(se_aux, c.val(se_aux))


def bcast_scalar(c, src11, dst_col, bank, cst, sem):
    t, v = c.t, c.v
    se_mm, se_ev = sem["mm"], sem["ev"]
    t.wait_ge(sem["aux"], c.val(sem["aux"]))
    t.wait_ge(se_ev, c.val(se_ev))
    c.inc(t.matmul(bank[:, 0:1], cst["onesrow_f32"], src11, start=True,
                   stop=True), se_mm)
    v.wait_ge(se_mm, c.val(se_mm))
    c.inc(v.tensor_copy(dst_col, bank[:, 0:1]), se_ev)


def dump_dbg(c, dbg, name, h, sem):
    if dbg and name in dbg:
        join_all(c, sem)
        c.inc(c.y.dma_start(out=dbg[name], in_=h[:]), sem["dma"], 16)
        c.y.wait_ge(sem["dma"], c.val(sem["dma"]))



def declare_stage2_io(nc):
    ap = {}

    def ein(name, shape, dt):
        ap[name] = nc.dram_tensor(name, shape, dt, kind="ExternalInput").ap()

    ein("xt_tok", [D, NTOK], F32)
    ein("wo", [2, 8, 128, D], F32)
    ein("w0t", [D, D], BF16)
    ein("b0r", [128, 16], BF16)
    ein("w1t", [D, D], F32)
    ein("d1t", [D, D], F32)
    ein("wgp", [64, 16, 128, 128], F32)
    ein("wup", [64, 16, 128, 128], F32)
    ein("wgup", [64, 2, 16, 128, 128], F32)
    ein("wdp", [64, 128, D], F32)
    ap["out"] = nc.dram_tensor("out", [D, NTOK], F32,
                               kind="ExternalOutput").ap()
    return ap


def emit_stage2(c, ap, z_src, cst, banks, dbg=None, pool=None,
                wo_prefetched=False):
    nc = c.nc
    t, v, s, g, y = c.t, c.v, c.s, c.g, c.y

    if pool is None:
        pool = [c.sb("pool%d" % i, [128, sz], BF16)
                for i, sz in enumerate(POOL_SIZES)]
    h = c.sb("h", [128, DC * NTOK], F32)
    sc = c.sb("sc", [128, 8], F32)
    s11 = c.sb("s11", [1, 16], F32)
    bias_bf = c.sb("bias_bf", [128, 16], BF16)
    bias_f = c.sb("bias_f", [128, 16], F32)

    pst, pbc = banks[6], banks[7]

    sem = make_sems(c, "s2")
    se_dma, se_wdma = sem["dma"], sem["wdma"]
    se_mm, se_ev, se_aux, se_cp = sem["mm"], sem["ev"], sem["aux"], sem["cp"]

    with c.scope():
        zt = c.sb("zt", [128, DC * NTOK], BF16)
        if not wo_prefetched:
            wo_r = ap["wo"].rearrange("hh j p n -> p (hh j) n")
            kc0 = 0
            for buf, sz in zip(pool, POOL_SIZES):
                nkc = sz // D
                c.inc(g.dma_start(
                    out=buf[:, 0:nkc * D].rearrange("p (a n) -> p a n", a=nkc),
                    in_=wo_r[:, kc0:kc0 + nkc, :]), se_wdma, 16)
                kc0 += nkc
        c.inc(y.dma_start(
            out=h[:].rearrange("p (a n) -> p a n", a=DC),
            in_=ap["xt_tok"].rearrange("(a p) n -> p a n", p=128)), se_dma, 16)
        for hh in range(2):
            c.inc(y.dma_start(
                out=zt[:].rearrange("p (a n) -> p a n", a=DC)[
                    :, hh * 8:(hh + 1) * 8, :],
                in_=z_src["z%d" % hh].rearrange("(j p) n -> p j n", p=128)),
                se_dma, 16)
        join_all(c, sem)

        def wo_lhsT(kc, ms):
            return pool_chunk(pool, kc, D)[:, ms * 128:(ms + 1) * 128]

        def wo_rhs(kc, tb):
            return zt[:, kc * NTOK + tb * 512: kc * NTOK + (tb + 1) * 512]

        def wo_evac(ms, tb, bank):
            v.wait_ge(se_mm, c.val(se_mm))
            dst = h[:, ms * NTOK + tb * 512: ms * NTOK + (tb + 1) * 512]
            c.inc(v.tensor_tensor(dst, dst, bank[:, 0:512], ALU.add), se_ev)

        emit_proj_fm(c, wo_lhsT, wo_rhs, wo_evac, banks[0:4], sem)
        join_all(c, sem)
        dump_dbg(c, dbg, "h1", h, sem)
        emit_ln_fm(c, h, cst, banks[4:8], sem)
        join_all(c, sem)
        dump_dbg(c, dbg, "h1n", h, sem)

    with c.scope():
        p2_inner = ExitStack()
        c._scopes.append(p2_inner)
        wq8 = c.sb("wq8", [128, DC * D], FP8)
        def red_h(amax):
            for kc in range(DC):
                c.inc(v.tensor_reduce(
                    amax[:, kc:kc + 1], h[:, kc * NTOK:(kc + 1) * NTOK],
                    AX.X, ALU.max, apply_absolute_value=True), se_aux)

        partition_absmax(c, red_h, DC, sem, "xs", s11[0:1, 0:1])
        c.inc(v.tensor_scalar(s11[0:1, 0:1], s11[0:1, 0:1], EPS_ATTN, None,
                              ALU.max), se_aux)
        armax_in = nc.dram_tensor("armax_in", [1, 8], F32).ap()
        armax_out = nc.dram_tensor("armax_out", [1, 8], F32).ap()
        v.memset(s11[0:1, 8:16], 0.0)
        v.wait_ge(se_aux, c.val(se_aux))
        c.inc(v.tensor_copy(s11[0:1, 8:9], s11[0:1, 0:1]), se_aux)
        g.wait_ge(se_aux, c.val(se_aux))
        c.inc(g.dma_start(out=armax_in, in_=s11[0:1, 8:16]), se_wdma, 16)
        g.wait_ge(se_wdma, c.val(se_wdma))
        c.inc(g.collective_compute(
            "AllReduce", ALU.max, replica_groups=[list(range(NCORE))],
            ins=[armax_in], outs=[armax_out]), sem["cc"])
        g.wait_ge(sem["cc"], c.val(sem["cc"]))
        c.inc(g.dma_start(out=s11[0:1, 1:2], in_=armax_out[0:1, 0:1]),
              se_wdma, 16)

        with c.scope():
            join_all(c, sem)
            pool_dma_loads(c, y, pool, ap["w0t"], DC, D, se_dma, False)
            c.inc(y.dma_start(out=bias_bf[:], in_=ap["b0r"]), se_dma, 16)
            v.wait_ge(se_dma, c.val(se_dma))
            c.inc(v.tensor_copy(bias_f[:], bias_bf[:]), se_aux)

            def red_w(amax):
                for kc in range(DC):
                    c.inc(v.tensor_reduce(
                        amax[:, kc:kc + 1], pool_chunk(pool, kc, D),
                        AX.X, ALU.max, apply_absolute_value=True), se_aux)

            partition_absmax(c, red_w, DC, sem, "ws", s11[0:1, 2:3])
            c.inc(v.tensor_scalar(s11[0:1, 2:3], s11[0:1, 2:3], EPS_ATTN,
                                  None, ALU.max), se_aux)
            v.wait_ge(se_aux, c.val(se_aux))
            c.inc(v.reciprocal(s11[0:1, 3:4], s11[0:1, 2:3]), se_aux)
            bcast_scalar(c, s11[0:1, 3:4], sc[:, 1:2], pbc, cst, sem)
            v.wait_ge(se_ev, c.val(se_ev))
            for kc in range(DC):
                c.inc(v.tensor_scalar(wq8[:, kc * D:(kc + 1) * D],
                                      pool_chunk(pool, kc, D),
                                      sc[:, 1:2], None, ALU.mult), se_aux)

        v.wait_ge(se_wdma, c.val(se_wdma))
        c.inc(v.reciprocal(s11[0:1, 4:5], s11[0:1, 1:2]), se_aux)
        v.wait_ge(se_aux, c.val(se_aux))
        c.inc(v.tensor_tensor(s11[0:1, 5:6], s11[0:1, 1:2], s11[0:1, 2:3],
                              ALU.mult), se_aux)
        bcast_scalar(c, s11[0:1, 4:5], sc[:, 0:1], pbc, cst, sem)
        bcast_scalar(c, s11[0:1, 5:6], sc[:, 2:3], pbc, cst, sem)
        xq8 = c.sb("xq8", [128, DC * NTOK], FP8)
        v.wait_ge(se_ev, c.val(se_ev))
        for kc in range(DC):
            c.inc(v.tensor_scalar(xq8[:, kc * NTOK:(kc + 1) * NTOK],
                                  h[:, kc * NTOK:(kc + 1) * NTOK],
                                  sc[:, 0:1], None, ALU.mult), se_aux)
        join_all(c, sem)

        ybf = c.sb("ybf", [128, 512], BF16)
        ybf2 = c.sb("ybf2", [128, 512], BF16)

        def f0_lhsT(kc, ms):
            return wq8[:, kc * D + ms * 128: kc * D + (ms + 1) * 128]

        def f0_rhs(kc, tb):
            return xq8[:, kc * NTOK + tb * 512: kc * NTOK + (tb + 1) * 512]

        def f0_evac(ms, tb, bank):
            v.wait_ge(se_mm, c.val(se_mm))
            v.tensor_scalar(ybf[:], bank[:, 0:512], sc[:, 2:3], None, ALU.mult)
            v.tensor_scalar(ybf2[:], ybf[:], bias_f[:, ms:ms + 1], None,
                            ALU.add)
            dst = h[:, ms * NTOK + tb * 512: ms * NTOK + (tb + 1) * 512]
            c.inc(v.tensor_tensor(dst, dst, ybf2[:], ALU.add), se_ev)

        emit_proj_fm(c, f0_lhsT, f0_rhs, f0_evac, banks[0:4], sem)
        join_all(c, sem)
        p2_inner.close()
        c._scopes.pop()
        dump_dbg(c, dbg, "h2", h, sem)
        emit_ln_fm(c, h, cst, banks[4:8], sem)
        join_all(c, sem)
        dump_dbg(c, dbg, "h2n", h, sem)

    with c.scope():
        h2nb = c.sb("h2nb", [128, DC * NTOK], BF16)
        for kc in range(DC):
            c.inc(v.tensor_copy(h2nb[:, kc * NTOK:(kc + 1) * NTOK],
                                h[:, kc * NTOK:(kc + 1) * NTOK]), se_cp)
        join_all(c, sem)
        wtmp = c.sb("wtmp", [128, 2 * D], BF16)
        w1src = ap["pre"]["w1t"] if "pre" in ap else None
        d1src = ap["pre"]["d1t"] if "pre" in ap else None
        for kc in range(DC):
            pp = kc % 2
            if kc >= 2:
                y.wait_ge(se_aux, c.val(se_aux) - 1)
            if w1src is not None:
                c.inc(y.dma_start(out=wtmp[:, pp * D:pp * D + D],
                                  in_=w1src[kc * 128:(kc + 1) * 128, :]),
                      se_wdma, 16)
                c.inc(y.dma_start(out=pool_chunk(pool, kc, D),
                                  in_=d1src[kc * 128:(kc + 1) * 128, :]),
                      se_wdma, 16)
            else:
                c.inc(g.dma_start(out=wtmp[:, pp * D:pp * D + D],
                                  in_=ap["w1t"][kc * 128:(kc + 1) * 128, :]),
                      se_wdma, 16)
                c.inc(g.dma_start(out=pool_chunk(pool, kc, D),
                                  in_=ap["d1t"][kc * 128:(kc + 1) * 128, :]),
                      se_wdma, 16)
            v.wait_ge(se_wdma, c.val(se_wdma))
            c.inc(v.tensor_tensor(pool_chunk(pool, kc, D),
                                  pool_chunk(pool, kc, D),
                                  wtmp[:, pp * D:pp * D + D], ALU.add), se_aux)
        join_all(c, sem)

        def l1_lhsT(kc, ms):
            return pool_chunk(pool, kc, D)[:, ms * 128:(ms + 1) * 128]

        def l1_rhs(kc, tb):
            return h2nb[:, kc * NTOK + tb * 512: kc * NTOK + (tb + 1) * 512]

        def l1_evac(ms, tb, bank):
            v.wait_ge(se_mm, c.val(se_mm))
            dst = h[:, ms * NTOK + tb * 512: ms * NTOK + (tb + 1) * 512]
            c.inc(v.tensor_tensor(dst, dst, bank[:, 0:512], ALU.add), se_ev)

        emit_proj_fm(c, l1_lhsT, l1_rhs, l1_evac, banks[0:4], sem)
        join_all(c, sem)
        dump_dbg(c, dbg, "h3", h, sem)

    with c.scope():
        h3nb = c.sb("h3nb", [128, DC * NTOK], BF16)
        emit_ln_fm(c, h, cst, banks[4:8], sem, hb_out=h3nb)
        join_all(c, sem)
        dump_dbg(c, dbg, "h3n", h, sem)

        p4_inner = ExitStack()
        self_scopes = c._scopes
        self_scopes.append(p4_inner)
        join_all(c, sem)
        use_pre = "pre" in ap
        hm4 = c.sb("hm4", [128, 4 * 512], F32)
        ab4 = c.sb("ab4", [128, 4 * 512], F32)
        tt1 = c.sb("tt1", [128, 512], F32)
        tt2 = c.sb("tt2", [128, 512], F32)
        tt3 = c.sb("tt3", [128, 512], F32)
        sg = c.sb("sg", [128, 2 * 512], F32)

        for th in range(2):
            tok0 = th * 512
            seq = [(gb, mm) for gb in range(16) for mm in range(4)]

            es_g = ExitStack()
            self_scopes.append(es_g)
            wgu = c.sb("wgu", [128, 2 * 2 * D], BF16)

            DEPTH = 2

            def p4w_load(k):
                gb_, mm_ = seq[k]
                m_ = mm_ * 16 + gb_
                sl = (k % DEPTH) * 2 * D
                if use_pre:
                    c.inc(g.dma_start(
                        out=wgu[:, sl:sl + 2 * D].rearrange(
                            "p (b a n) -> p b a n", b=2, a=16),
                        in_=ap["pre"]["wgup"][m_].rearrange(
                            "b a p n -> p b a n")), se_wdma, 16)
                else:
                    c.inc(g.dma_start(
                        out=wgu[:, sl:sl + D].rearrange(
                            "p (a n) -> p a n", a=16),
                        in_=ap["wgp"][m_].rearrange("a p n -> p a n")),
                        se_wdma, 16)
                    c.inc(g.dma_start(
                        out=wgu[:, sl + D:sl + 2 * D].rearrange(
                            "p (a n) -> p a n", a=16),
                        in_=ap["wup"][m_].rearrange("a p n -> p a n")),
                        se_wdma, 16)
                return c.val(se_wdma)

            wsnap = [p4w_load(kk) for kk in range(DEPTH)]
            esnap, qsnap, msnap = [], [], []
            for k, (gb, mm) in enumerate(seq):
                m = mm * 16 + gb
                sl = (k % DEPTH) * 2 * D
                t.wait_ge(se_wdma, wsnap[k])
                if k >= 2:
                    t.wait_ge(se_ev, esnap[k - 2])
                    t.wait_ge(sem["sq"], qsnap[k - 2])
                pg = banks[2 * (k % 2)]
                pu = banks[2 * (k % 2) + 1]
                i = None
                for kc in range(DC):
                    t.matmul(
                        pg[:, 0:512],
                        wgu[:, sl + kc * 128: sl + (kc + 1) * 128],
                        h3nb[:, kc * NTOK + tok0: kc * NTOK + tok0 + 512],
                        start=(kc == 0), stop=(kc == DC - 1))
                    i = t.matmul(
                        pu[:, 0:512],
                        wgu[:, sl + D + kc * 128: sl + D + (kc + 1) * 128],
                        h3nb[:, kc * NTOK + tok0: kc * NTOK + tok0 + 512],
                        start=(kc == 0), stop=(kc == DC - 1))
                c.inc(i, se_mm)
                msnap.append(c.val(se_mm))
                if k + DEPTH < len(seq):
                    g.wait_ge(se_mm, msnap[k])
                    wsnap.append(p4w_load(k + DEPTH))
                s.wait_ge(se_mm, c.val(se_mm))
                c.inc(s.activation(sg[:, (k % 2) * 512:(k % 2) * 512 + 512],
                                   pg[:, 0:512], AF.Silu), sem["sq"])
                qsnap.append(c.val(sem["sq"]))
                v.wait_ge(se_mm, c.val(se_mm))
                v.wait_ge(sem["sq"], c.val(sem["sq"]))
                c.inc(v.tensor_tensor(hm4[:, mm * 512:(mm + 1) * 512],
                                      sg[:, (k % 2) * 512:(k % 2) * 512 + 512],
                                      pu[:, 0:512], ALU.mult), se_ev)
                esnap.append(c.val(se_ev))
                if mm == 3:
                    s.wait_ge(se_ev, c.val(se_ev))
                    for mm2 in range(4):
                        c.inc(s.activation(ab4[:, mm2 * 512:(mm2 + 1) * 512],
                                           hm4[:, mm2 * 512:(mm2 + 1) * 512],
                                           AF.Abs), sem["sq"])
                    v.wait_ge(sem["sq"], c.val(sem["sq"]))
                    A = [ab4[:, x * 512:(x + 1) * 512] for x in range(4)]
                    Hm = [hm4[:, x * 512:(x + 1) * 512] for x in range(4)]
                    v.tensor_tensor(tt1[:], A[0], A[1], ALU.max)
                    v.tensor_tensor(tt2[:], A[2], A[3], ALU.max)
                    v.tensor_tensor(tt1[:], tt1[:], tt2[:], ALU.min)
                    v.tensor_tensor(tt2[:], A[0], A[1], ALU.min)
                    v.tensor_tensor(tt3[:], A[2], A[3], ALU.min)
                    v.tensor_tensor(tt2[:], tt2[:], tt3[:], ALU.max)
                    v.tensor_tensor(tt1[:], tt1[:], tt2[:], ALU.max)
                    for mm2 in range(4):
                        m2 = mm2 * 16 + gb
                        v.tensor_tensor(tt2[:], A[mm2], tt1[:], ALU.is_ge)
                        c.inc(v.tensor_tensor(
                            pool_chunk(pool, m2, 512), Hm[mm2], tt2[:],
                            ALU.mult), se_ev)
                    esnap[-1] = c.val(se_ev)
                    qsnap[-1] = c.val(sem["sq"])
            join_all(c, sem)
            es_g.close()
            self_scopes.pop()
            es_d = ExitStack()
            self_scopes.append(es_d)
            wdt = c.sb("wdt", [128, 2 * D], BF16)

            wdsrc = ap["pre"]["wdp"] if use_pre else None
            for msg in range(2):
                def p4d_load(mp_):
                    sl_ = (mp_ % 2) * D
                    if wdsrc is not None:
                        c.inc(g.dma_start(
                            out=wdt[:, sl_:sl_ + D].rearrange(
                                "p (m n) -> p m n", m=2),
                            in_=wdsrc[2 * mp_:2 * mp_ + 2, :,
                                      msg * NTOK:(msg + 1) * NTOK].rearrange(
                                "m p n -> p m n")), se_wdma, 16)
                    else:
                        c.inc(g.dma_start(
                            out=wdt[:, sl_:sl_ + D].rearrange(
                                "p (m n) -> p m n", m=2),
                            in_=ap["wdp"][2 * mp_:2 * mp_ + 2, :,
                                          msg * NTOK:(msg + 1) * NTOK].rearrange(
                                "m p n -> p m n")), se_wdma, 16)
                    return c.val(se_wdma)

                dw = [p4d_load(0), p4d_load(1)]
                dm = []
                for mp in range(32):
                    sl_ = (mp % 2) * D
                    t.wait_ge(se_wdma, dw[mp])
                    i = None
                    for m2 in range(2):
                        m = 2 * mp + m2
                        for ms8 in range(8):
                            i = t.matmul(
                                banks[ms8][:, 0:512],
                                wdt[:, sl_ + m2 * NTOK + ms8 * 128:
                                    sl_ + m2 * NTOK + (ms8 + 1) * 128],
                                pool_chunk(pool, m, 512),
                                start=(m == 0), stop=(m == 63))
                    c.inc(i, se_mm)
                    dm.append(c.val(se_mm))
                    if mp + 2 < 32:
                        g.wait_ge(se_mm, dm[mp])
                        dw.append(p4d_load(mp + 2))
                v.wait_ge(se_mm, c.val(se_mm))
                for ms8 in range(8):
                    msx = msg * 8 + ms8
                    dst = h[:, msx * NTOK + tok0: msx * NTOK + tok0 + 512]
                    c.inc(v.tensor_tensor(dst, dst, banks[ms8][:, 0:512],
                                          ALU.add), se_ev)
                join_all(c, sem)
            es_d.close()
            self_scopes.pop()
        p4_inner.close()
        self_scopes.pop()
        dump_dbg(c, dbg, "h4", h, sem)
        emit_ln_fm(c, h, cst, banks[4:8], sem)
        join_all(c, sem)

    c.inc(y.dma_start(
        out=ap["out"].rearrange("(a p) n -> p a n", p=128),
        in_=h[:].rearrange("p (a n) -> p a n", a=DC)), sem["dma"], 16)
    y.wait_ge(sem["dma"], c.val(sem["dma"]))


def se_sq_g(sem):
    return sem["sq"]



def launch_a2a(c, hh, a2a_in, a2a_out, zT, sem):
    g = c.g
    for b in range(2):
        pair = b * 2 + hh
        c.inc(g.dma_start(
            out=a2a_in[hh][b * 512:(b + 1) * 512, :].rearrange(
                "(r p) n -> p r n", p=128),
            in_=zT[:, pair * N:(pair + 1) * N].rearrange(
                "p (m n) -> p m n", m=4)), sem["dma"], 16)
    g.wait_ge(sem["dma"], c.val(sem["dma"]))
    with allow_ici_all_to_all():
        c.inc(g.collective_compute(
            "AllToAll", ALU.bypass,
            replica_groups=[list(range(NCORE))],
            ins=[a2a_in[hh]], outs=[a2a_out[hh]]), sem["cc"])


def declare_stage1_io(nc, external_z):
    ap = {}

    def ein(name, shape, dt):
        ap[name] = nc.dram_tensor(name, shape, dt, kind="ExternalInput").ap()

    ein("xtt", [B, 8, 128, 16, 512], F32)
    ein("wq", [D, 256], F32)
    ein("wk", [D, 256], F32)
    ein("wvk", [D, 512], F32)
    ein("id128", [128, 128], BF16)
    ein("mau", [128, 128], BF16)
    ein("my0", [128, 128], BF16)
    ein("minc", [128, 128], BF16)
    ein("mpt", [4, 128, 512], BF16)
    if external_z:
        for hh in range(2):
            ap["z%d" % hh] = nc.dram_tensor(
                "z%d" % hh, [NTOK, NTOK], BF16, kind="ExternalOutput").ap()
    return ap


SQ_DH = 0.08838834764831845
DEBUG_JOIN_CHUNK = 0
OPT_CHAIN_ACT = False
OPT_PREP_ACTRECIP = True
OPT_PROJ_ALT = True

POOL_SIZES = [16384, 8192, 4096, 4096]


def pool_chunk(pool, idx, width):
    o = idx * width
    for buf, sz in zip(pool, POOL_SIZES):
        if o < sz:
            assert o + width <= sz
            return buf[:, o:o + width]
        o -= sz
    raise IndexError


def pool_dma_loads(c, eng, pool, in_2d, n_chunks, chunk_elems, sem, casted):
    src = in_2d.rearrange("(a p) n -> p a n", p=128)
    kc0 = 0
    for buf, sz in zip(pool, POOL_SIZES):
        nkc = sz // chunk_elems
        nkc = min(nkc, n_chunks - kc0)
        if nkc <= 0:
            break
        c.inc(eng.dma_start(
            out=buf[:, 0:nkc * chunk_elems].rearrange(
                "p (a n) -> p a n", a=nkc),
            in_=src[:, kc0:kc0 + nkc, :]), sem, 16)
        kc0 += nkc


def emit_stage1(c, ap, cst, banks, dbg=None):
    nc = c.nc
    t, v, s, g, y = c.t, c.v, c.s, c.g, c.y
    sem = make_sems(c, "s1")
    se_mm, se_ev, se_aux = sem["mm"], sem["ev"], sem["aux"]
    se_cp, se_sq, se_dma, se_wdma = sem["cp"], sem["sq"], sem["dma"], sem["wdma"]
    se_xb = c.sem("s1xb")
    se_cc = sem["cc"]

    a2a_in = [nc.dram_tensor("a2a_in%d" % hh, [8 * 128, NTOK], BF16).ap()
              for hh in range(2)]
    a2a_out = [nc.dram_tensor("a2a_out%d" % hh, [NTOK, NTOK], BF16).ap()
               for hh in range(2)]

    wqb = c.sb("wqb", [128, 16 * 256], BF16)
    wkb = c.sb("wkb", [128, 16 * 256], BF16)
    wvkb = c.sb("wvkb", [128, 16 * 512], BF16)
    xb = c.sb("xb", [128, 2 * 16 * 512], BF16)
    inner = ExitStack()
    c._scopes.append(inner)
    qTf = c.sb("qTf", [128, 2 * N], BF16)
    kTf = c.sb("kTf", [128, 2 * N], BF16)
    vtm = c.sb("vtm", [128, 64 * 256], BF16)
    zT = c.sb("zTs", [128, 4 * N], BF16)
    id128b = c.sb("id128b", [128, 128], BF16)
    maub = c.sb("maub", [128, 128], BF16)
    my0b = c.sb("my0b", [128, 128], BF16)
    mincb = c.sb("mincb", [128, 128], BF16)
    mptb = c.sb("mptb", [128, 4 * 512], BF16)
    ktm4 = c.sb("ktm4", [128, 2 * 4 * 256], BF16)

    EB = c.sb("EB", [128, 2 * 512], BF16)
    QKl = c.sb("QKl", [128, 2 * 512], BF16)
    csrow = c.sb("csrow", [1, 512], F32)
    csbf = c.sb("csbf", [1, 2 * 512], BF16)
    Ek = c.sb("Ek", [128, 2 * 256], BF16)
    sum2 = c.sb("sum2", [128, 2], F32)
    rcp2 = c.sb("rcp2", [128, 2], F32)
    klt_tm = c.sb("klt_tm", [128, 2 * 256], BF16)
    tA = c.sb("tA", [128, 256], F32)
    tBv = c.sb("tB", [128, 2 * 256], F32)
    dtok = c.sb("dtok", [128, 256], F32)
    dTf = c.sb("dTf", [128, 256], F32)
    vp_tm = c.sb("vp_tm", [128, 2 * 256], BF16)
    kb_tm = c.sb("kb_tm", [128, 2 * 256], BF16)
    kbT = c.sb("kbT", [128, 2 * 256], BF16)
    Au = [c.sb("Au", [128, 128], BF16) for _ in range(2)]
    Ybuf = [c.sb("Ybuf", [128, 128], BF16) for _ in range(2)]
    chainB = [c.sb("chainB", [128, 256], BF16) for _ in range(2)]
    Tt = [c.sb("Tt", [128, 128], BF16) for _ in range(2)]
    nwT = [c.sb("nwT", [128, 128], BF16) for _ in range(2)]
    up_tm = [c.sb("up_tm", [128, 128], BF16) for _ in range(2)]
    atT = [c.sb("atT", [128, 128], BF16) for _ in range(2)]
    S_sb = c.sb("S_sb", [128, 2 * 128], BF16)
    pt = c.sb("ptb", [128, 2 * 512], BF16)
    rB = c.sb("rBb", [128, 512], BF16)
    tmpO = c.sb("tmpO", [128, 512], BF16)
    mmsnap = {}
    s_free = [0, 0]
    chain_free = [0, 0]
    z_free = [0, 0]

    c.inc(y.dma_start(out=id128b[:], in_=ap["id128"]), se_dma, 16)
    c.inc(y.dma_start(out=maub[:], in_=ap["mau"]), se_dma, 16)
    c.inc(y.dma_start(out=my0b[:], in_=ap["my0"]), se_dma, 16)
    c.inc(y.dma_start(out=mincb[:], in_=ap["minc"]), se_dma, 16)
    c.inc(y.dma_start(out=mptb[:].rearrange("p (i n) -> p i n", i=4),
                      in_=ap["mpt"].rearrange("i p n -> p i n")), se_dma, 16)
    c.inc(g.dma_start(out=wqb[:].rearrange("p (a n) -> p a n", a=16),
                      in_=ap["wq"].rearrange("(a p) n -> p a n", p=128)),
          se_wdma, 16)
    c.inc(g.dma_start(out=wkb[:].rearrange("p (a n) -> p a n", a=16),
                      in_=ap["wk"].rearrange("(a p) n -> p a n", p=128)),
          se_wdma, 16)
    c.inc(g.dma_start(out=wvkb[:].rearrange("p (a n) -> p a n", a=16),
                      in_=ap["wvk"].rearrange("(a p) n -> p a n", p=128)),
          se_wdma, 16)
    join_all(c, sem)

    S_banks = (banks[6], banks[7])

    def xb_load(b_, tg_, slot):
        c.inc(g.dma_start(
            out=xb[:, slot * 8192:(slot + 1) * 8192].rearrange(
                "p (a n) -> p a n", a=16),
            in_=ap["xtt"][b_, tg_]), se_wdma, 16)

    xb_load(0, 0, 0)

    phase_floor = 0
    tile_mm_done = {}
    for b in range(2):
        phase_floor = c.val(se_mm)
        for tg in range(8):
            tile_no = b * 8 + tg
            pp = tile_no % 2
            xv = xb[:, pp * 8192:(pp + 1) * 8192]
            if tile_no + 1 < 16:
                nb_, ntg_ = divmod(tile_no + 1, 8)
                if tile_no >= 1:
                    g.wait_ge(se_mm, tile_mm_done[tile_no - 1])
                xb_load(nb_, ntg_, (tile_no + 1) % 2)
            t.wait_ge(se_wdma, 48 + 16 * (tile_no + 1))

            if DEBUG_JOIN_CHUNK == 3:
                join_all(c, sem)
            t.wait_ge(se_aux, c.val(se_aux))
            t.wait_ge(se_ev, c.val(se_ev))
            t.wait_ge(se_sq, c.val(se_sq))
            pj_ev = []
            job = 0
            for j, wsb in enumerate((wqb, wkb)):
                for hh in range(2):
                    bank = banks[4 + job % 2]
                    if job >= 2:
                        t.wait_ge(se_ev, pj_ev[job - 2])
                    i = None
                    for kc in range(DC):
                        i = t.matmul(
                            bank[:, 0:512],
                            wsb[:, kc * 256 + hh * 128: kc * 256 + (hh + 1) * 128],
                            xv[:, kc * 512:(kc + 1) * 512],
                            start=(kc == 0), stop=(kc == DC - 1))
                    c.inc(i, se_mm)
                    dstf = qTf if j == 0 else kTf
                    dv_ = dstf[:, hh * N + tg * 512: hh * N + tg * 512 + 512]
                    if job % 2 == 0 or not OPT_PROJ_ALT:
                        v.wait_ge(se_mm, c.val(se_mm))
                        c.inc(v.tensor_copy(dv_, bank[:, 0:512]), se_ev)
                    else:
                        s.wait_ge(se_mm, c.val(se_mm))
                        c.inc(s.activation(dv_, bank[:, 0:512], AF.Copy),
                              se_ev)
                    pj_ev.append(c.val(se_ev))
                    job += 1
            for tt in range(4):
                bank = banks[4 + job % 2]
                t.wait_ge(se_ev, pj_ev[job - 2])
                i = None
                for kc in range(DC):
                    i = t.matmul(
                        bank[:, 0:512],
                        xv[:, kc * 512 + tt * 128: kc * 512 + (tt + 1) * 128],
                        wvkb[:, kc * 512:(kc + 1) * 512],
                        start=(kc == 0), stop=(kc == DC - 1))
                c.inc(i, se_mm)
                v.wait_ge(se_mm, c.val(se_mm))
                c.inc(v.tensor_copy(
                    vtm[:, ((b * 8 + tg) * 4 + tt) * 256:
                        ((b * 8 + tg) * 4 + tt) * 256 + 256],
                    bank[:, 0:256]), se_ev)
                c.inc(v.tensor_copy(
                    ktm4[:, (tg % 2) * 1024 + tt * 256:
                         (tg % 2) * 1024 + (tt + 1) * 256],
                    bank[:, 256:512]), se_ev)
                pj_ev.append(c.val(se_ev))
                job += 1
            tile_mm_done[tile_no] = c.val(se_mm)
            proj_done = c.val(se_ev)

            for tt in range(4):
                ch = tg * 4 + tt
                gch = b * 32 + ch
                par = gch % 2
                kraw = ktm4[:, (tg % 2) * 1024 + tt * 256:
                            (tg % 2) * 1024 + (tt + 1) * 256]
                vsl = vtm[:, ((b * 8 + tg) * 4 + tt) * 256:
                          ((b * 8 + tg) * 4 + tt) * 256 + 256]
                EBp = EB[:, par * 512:(par + 1) * 512]
                QKp = QKl[:, par * 512:(par + 1) * 512]
                kbTp = kbT[:, par * 256:(par + 1) * 256]
                Ekp = Ek[:, par * 256:(par + 1) * 256]
                kltp = klt_tm[:, par * 256:(par + 1) * 256]
                vpp = vp_tm[:, par * 256:(par + 1) * 256]
                kbp = kb_tm[:, par * 256:(par + 1) * 256]
                tBp = tBv[:, par * 256:(par + 1) * 256]
                csbp = csbf[:, par * 512:(par + 1) * 512]

                lag = max(mmsnap.get(gch - 2, 0), phase_floor)
                s.wait_ge(se_mm, lag)
                v.wait_ge(se_mm, lag)
                s.wait_ge(se_ev, proj_done)
                v.wait_ge(se_ev, proj_done)

                c.inc(s.activation(Ekp, kraw, AF.Exp), se_sq)
                v.wait_ge(se_sq, c.val(se_sq))
                c.inc(v.tensor_reduce(
                    sum2[:], Ekp.rearrange("p (i n) -> p i n", i=2),
                    AX.X, ALU.add), se_aux)
                v.wait_ge(se_aux, c.val(se_aux))
                c.inc(v.reciprocal(rcp2[:], sum2[:]), se_aux)
                v.wait_ge(se_aux, c.val(se_aux))
                for hh in range(2):
                    v.tensor_scalar(
                        kltp[:, hh * 128:(hh + 1) * 128],
                        Ekp[:, hh * 128:(hh + 1) * 128],
                        rcp2[:, hh:hh + 1], 1.0 - EPS_ATTN, ALU.mult, ALU.min)
                    c.inc(v.tensor_scalar(
                        kltp[:, hh * 128:(hh + 1) * 128],
                        kltp[:, hh * 128:(hh + 1) * 128],
                        EPS_ATTN, None, ALU.max), se_aux)
                c.inc(s.activation(tA[:], kraw, AF.Abs), se_sq)
                s.wait_ge(se_sq, c.val(se_sq))
                c.inc(s.activation(tBp, tA[:], AF.Exp, scale=-1.0), se_sq)
                s.wait_ge(se_sq, c.val(se_sq))
                c.inc(s.activation(tBp, tBp, AF.Ln, bias=cst["one_p"]), se_sq)
                v.tensor_scalar(dtok[:], kraw, 0.0, None, ALU.min)
                v.wait_ge(se_sq, c.val(se_sq))
                v.tensor_tensor(dtok[:], dtok[:], tBp, ALU.subtract)
                c.inc(v.tensor_scalar(dtok[:], dtok[:], -256.0, 0.0,
                                      ALU.max, ALU.min), se_aux)
                c.inc(v.scalar_tensor_tensor(
                    vpp, dtok[:], 1.0 / GATE_NORM, vsl,
                    ALU.mult, ALU.mult), se_aux)
                c.inc(v.scalar_tensor_tensor(
                    kbp, dtok[:], 1.0 / GATE_NORM, kltp,
                    ALU.mult, ALU.mult), se_aux)

                for hh in range(2):
                    c.inc(s.activation(
                        EBp[:, hh * 128:(hh + 1) * 128],
                        qTf[:, hh * N + ch * 128: hh * N + (ch + 1) * 128],
                        AF.Exp), se_sq)
                    c.inc(s.activation(
                        EBp[:, 256 + hh * 128: 256 + (hh + 1) * 128],
                        kTf[:, hh * N + ch * 128: hh * N + (ch + 1) * 128],
                        AF.Exp), se_sq)
                t.wait_ge(se_sq, c.val(se_sq))
                t.wait_ge(se_aux, c.val(se_aux))
                t.wait_ge(se_ev, proj_done)
                c.inc(t.matmul(banks[4][0:1, 0:512], cst["onescol"], EBp,
                               start=True, stop=True), se_mm)
                if OPT_PREP_ACTRECIP:
                    s.wait_ge(se_mm, c.val(se_mm))
                    c.inc(s.activation(csrow[:], banks[4][0:1, 0:512], AF.Ln),
                          se_sq)
                    s.wait_ge(se_sq, c.val(se_sq))
                    c.inc(s.activation(csbp, csrow[:], AF.Exp, scale=-1.0),
                          se_sq)
                    t.wait_ge(se_sq, c.val(se_sq))
                else:
                    v.wait_ge(se_mm, c.val(se_mm))
                    c.inc(v.reciprocal(csrow[:], banks[4][0:1, 0:512]),
                          se_aux)
                    v.wait_ge(se_aux, c.val(se_aux))
                    c.inc(v.tensor_copy(csbp, csrow[:]), se_aux)
                    t.wait_ge(se_aux, c.val(se_aux))
                c.inc(t.matmul(banks[5][:, 0:512], cst["onesrow_bf16"],
                               csbp, start=True, stop=True), se_mm)
                v.wait_ge(se_mm, c.val(se_mm))
                v.tensor_tensor(QKp, EBp, banks[5][:, 0:512], ALU.mult)
                v.tensor_scalar(QKp, QKp, EPS_ATTN, 1.0 - EPS_ATTN,
                                ALU.max, ALU.min)
                c.inc(v.tensor_scalar(QKp[:, 0:256], QKp[:, 0:256], SQ_DH,
                                      None, ALU.mult), se_aux)
                s.wait_ge(se_aux, c.val(se_aux))
                for hh in range(2):
                    kv = kTf[:, hh * N + ch * 128: hh * N + (ch + 1) * 128]
                    c.inc(s.activation(tA[:, hh * 128:(hh + 1) * 128], kv,
                                       AF.Abs), se_sq)
                s.wait_ge(se_sq, c.val(se_sq))
                c.inc(s.activation(tBp, tA[:], AF.Exp, scale=-1.0), se_sq)
                s.wait_ge(se_sq, c.val(se_sq))
                c.inc(s.activation(tBp, tBp, AF.Ln, bias=cst["one_p"]), se_sq)
                for hh in range(2):
                    kv = kTf[:, hh * N + ch * 128: hh * N + (ch + 1) * 128]
                    v.tensor_scalar(dTf[:, hh * 128:(hh + 1) * 128], kv,
                                    0.0, None, ALU.min)
                v.wait_ge(se_sq, c.val(se_sq))
                v.tensor_tensor(dTf[:], dTf[:], tBp, ALU.subtract)
                c.inc(v.tensor_scalar(dTf[:], dTf[:], -256.0, 0.0,
                                      ALU.max, ALU.min), se_aux)
                c.inc(v.scalar_tensor_tensor(
                    kbTp, dTf[:], 1.0 / GATE_NORM, QKp[:, 256:512],
                    ALU.mult, ALU.mult), se_aux)
                prep_aux = c.val(se_aux)

                qlT = [QKp[:, hh * 128:(hh + 1) * 128] for hh in range(2)]
                klT = [QKp[:, 256 + hh * 128: 256 + (hh + 1) * 128]
                       for hh in range(2)]
                kbTh = [kbTp[:, hh * 128:(hh + 1) * 128] for hh in range(2)]
                kbh = [kbp[:, hh * 128:(hh + 1) * 128] for hh in range(2)]
                vph = [vpp[:, hh * 128:(hh + 1) * 128] for hh in range(2)]
                klh = [kltp[:, hh * 128:(hh + 1) * 128] for hh in range(2)]
                Ssb = [S_sb[:, hh * 128:(hh + 1) * 128] for hh in range(2)]

                mm_h = [0, 0]
                ev_h = [0, 0]
                ysq_h = [0, 0]
                t.wait_ge(se_aux, prep_aux)
                for hh in range(2):
                    bk = banks[2 * hh]
                    if gch >= 1:
                        t.wait_ge(se_ev, chain_free[hh])
                        t.wait_ge(se_sq, z_free[hh])
                    t.matmul(bk[:, 0:128], klT[hh], kbTh[hh],
                             start=True, stop=True)
                    c.inc(t.matmul(bk[:, 128:256], kbTh[hh], klT[hh],
                                   start=True, stop=True), se_mm)
                    mm_h[hh] = c.val(se_mm)
                for hh in range(2):
                    bk = banks[2 * hh]
                    v.wait_ge(se_mm, mm_h[hh])
                    v.tensor_tensor(Au[hh][:], bk[:, 0:128], maub[:],
                                    ALU.mult)
                    v.tensor_tensor(Ybuf[hh][:], bk[:, 128:256], my0b[:],
                                    ALU.mult)
                    v.tensor_copy(chainB[hh][:, 0:128], id128b[:])
                    c.inc(v.tensor_copy(chainB[hh][:, 128:256], Au[hh][:]),
                          se_ev)
                    ev_h[hh] = c.val(se_ev)
                for lvl in range(7):
                    for hh in range(2):
                        pa = banks[2 * hh + (lvl + 1) % 2]
                        t.wait_ge(se_ev, ev_h[hh])
                        if ysq_h[hh]:
                            t.wait_ge(se_sq, ysq_h[hh])
                        if lvl < 6:
                            t.matmul(pa[:, 0:256], Ybuf[hh][:],
                                     chainB[hh][:, 0:256],
                                     start=True, stop=True)
                            c.inc(t.matmul(pa[:, 256:384],
                                           chainB[hh][:, 128:256],
                                           Ybuf[hh][:], start=True,
                                           stop=True), se_mm)
                        else:
                            c.inc(t.matmul(pa[:, 0:128], Ybuf[hh][:],
                                           chainB[hh][:, 0:128],
                                           start=True, stop=True), se_mm)
                        mm_h[hh] = c.val(se_mm)
                    for hh in range(2):
                        pa = banks[2 * hh + (lvl + 1) % 2]
                        v.wait_ge(se_mm, mm_h[hh])
                        if lvl < 6:
                            if OPT_CHAIN_ACT:
                                s.wait_ge(se_mm, mm_h[hh])
                                c.inc(s.activation(Ybuf[hh][:],
                                                   pa[:, 256:384],
                                                   AF.Copy), se_sq)
                                ysq_h[hh] = c.val(se_sq)
                            else:
                                c.inc(v.tensor_copy(Ybuf[hh][:],
                                                    pa[:, 256:384]), se_ev)
                            v.tensor_tensor(chainB[hh][:, 0:128],
                                            pa[:, 0:128],
                                            chainB[hh][:, 0:128], ALU.add)
                            c.inc(v.tensor_copy(chainB[hh][:, 128:256],
                                                pa[:, 128:256]), se_ev)
                        else:
                            c.inc(v.tensor_tensor(
                                Tt[hh][:], pa[:, 0:128],
                                chainB[hh][:, 0:128], ALU.add), se_ev)
                        ev_h[hh] = c.val(se_ev)

                sq_h = [0, 0]
                if ch > 0:
                    for hh in range(2):
                        t.wait_ge(se_ev, ev_h[hh])
                        t.wait_ge(se_ev, s_free[hh])
                        c.inc(t.matmul(banks[2 * hh][:, 0:128], kbh[hh],
                                       Tt[hh][:], start=True, stop=True),
                              se_mm)
                        mm_h[hh] = c.val(se_mm)
                    for hh in range(2):
                        s.wait_ge(se_mm, mm_h[hh])
                        c.inc(s.activation(nwT[hh][:],
                                           banks[2 * hh][:, 0:128],
                                           AF.Copy, scale=-1.0), se_sq)
                        sq_h[hh] = c.val(se_sq)
                for hh in range(2):
                    if ch > 0:
                        t.wait_ge(se_sq, sq_h[hh])
                    else:
                        t.wait_ge(se_ev, ev_h[hh])
                    bk1 = banks[2 * hh + 1]
                    t.matmul(bk1[:, 0:128], Tt[hh][:], vph[hh],
                             start=True, stop=(ch == 0))
                    if ch > 0:
                        c.inc(t.matmul(bk1[:, 0:128], nwT[hh][:], Ssb[hh],
                                       start=False, stop=True), se_mm)
                    else:
                        c.inc(t.nop(), se_mm)
                    mm_h[hh] = c.val(se_mm)
                for hh in range(2):
                    v.wait_ge(se_mm, mm_h[hh])
                    c.inc(v.tensor_copy(up_tm[hh][:],
                                        banks[2 * hh + 1][:, 0:128]), se_ev)
                    ev_h[hh] = c.val(se_ev)
                for hh in range(2):
                    c.inc(t.matmul(banks[2 * hh][:, 128:256], klT[hh],
                                   qlT[hh], start=True, stop=True), se_mm)
                    mm_h[hh] = c.val(se_mm)
                for hh in range(2):
                    v.wait_ge(se_mm, mm_h[hh])
                    c.inc(v.tensor_tensor(atT[hh][:],
                                          banks[2 * hh][:, 128:256],
                                          mincb[:], ALU.mult), se_ev)
                    ev_h[hh] = c.val(se_ev)
                for hh in range(2):
                    t.wait_ge(se_ev, ev_h[hh])
                    bk1 = banks[2 * hh + 1]
                    if ch > 0:
                        t.matmul(bk1[:, 128:256], Ssb[hh], qlT[hh],
                                 start=True, stop=False)
                    c.inc(t.matmul(bk1[:, 128:256], up_tm[hh][:], atT[hh][:],
                                   start=(ch == 0), stop=True), se_mm)
                    mm_h[hh] = c.val(se_mm)
                for hh in range(2):
                    s.wait_ge(se_mm, mm_h[hh])
                    c.inc(s.activation(
                        zT[:, (b * 2 + hh) * N + ch * 128:
                           (b * 2 + hh) * N + (ch + 1) * 128],
                        banks[2 * hh + 1][:, 128:256],
                        AF.Copy, scale=0.5), se_sq)
                    sq_h[hh] = c.val(se_sq)
                    z_free[hh] = c.val(se_sq)
                for hh in range(2):
                    t.wait_ge(se_ev, s_free[hh])
                    c.inc(t.matmul(S_banks[hh][:, 0:128], klh[hh],
                                   up_tm[hh][:], start=(ch == 0), stop=True),
                          se_mm)
                    mm_h[hh] = c.val(se_mm)
                for hh in range(2):
                    v.wait_ge(se_mm, mm_h[hh])
                    c.inc(v.tensor_copy(Ssb[hh], S_banks[hh][:, 0:128]),
                          se_ev)
                    s_free[hh] = c.val(se_ev)
                    chain_free[hh] = c.val(se_ev)
                mmsnap[gch] = c.val(se_mm)
                if DEBUG_JOIN_CHUNK == 1 or (DEBUG_JOIN_CHUNK == 2 and tt == 3):
                    join_all(c, sem)

        v.wait_ge(se_sq, c.val(se_sq))
        prev_ev = c.val(se_ev)
        prev_aux = c.val(se_aux)
        prev_sq = c.val(se_sq)
        for hh in range(2):
            pair = b * 2 + hh
            for qb in range(8):
                nkt = 4 * qb + 4
                qb_mm0 = c.val(se_mm)
                stm, esq, amm = [], [], []

                def emit_ST(kt):
                    pS = banks[kt % 2]
                    if kt >= 2:
                        sm_, vv_ = esq[kt - 2]
                        t.wait_ge(sm_, vv_)
                    c.inc(t.matmul(
                        pS[:, 0:512],
                        kTf[:, hh * N + kt * 128: hh * N + (kt + 1) * 128],
                        qTf[:, hh * N + qb * 512: hh * N + qb * 512 + 512],
                        start=True, stop=True), se_mm)
                    stm.append(c.val(se_mm))

                emit_ST(0)
                for kt in range(nkt):
                    if kt + 1 < nkt:
                        emit_ST(kt + 1)
                    ptv = pt[:, (kt % 2) * 512:(kt % 2) * 512 + 512]
                    s.wait_ge(se_mm, stm[kt])
                    s.wait_ge(se_mm, amm[kt - 2] if kt >= 2 else qb_mm0)
                    c.inc(s.activation(ptv, banks[kt % 2][:, 0:512], AF.Exp,
                                       scale=SQ_DH), se_sq)
                    if kt >= 4 * qb:
                        v.wait_ge(se_sq, c.val(se_sq))
                        v.wait_ge(se_mm, amm[kt - 2] if kt >= 2 else qb_mm0)
                        c.inc(v.tensor_tensor(
                            ptv, ptv,
                            mptb[:, (kt - 4 * qb) * 512:
                                 (kt - 4 * qb + 1) * 512], ALU.mult), se_aux)
                        t.wait_ge(se_aux, c.val(se_aux))
                        esq.append((se_aux, c.val(se_aux)))
                    else:
                        t.wait_ge(se_sq, c.val(se_sq))
                        esq.append((se_sq, c.val(se_sq)))
                    if kt == 0:
                        t.wait_ge(se_ev, prev_ev)
                        t.wait_ge(se_aux, prev_aux)
                        t.wait_ge(se_sq, prev_sq)
                    t.matmul(banks[2][:, 0:512],
                             vtm[:, (b * 32 + kt) * 256 + hh * 128:
                                 (b * 32 + kt) * 256 + (hh + 1) * 128],
                             ptv, start=(kt == 0), stop=(kt == nkt - 1))
                    c.inc(t.matmul(banks[3][0:1, 0:512], cst["onescol"], ptv,
                                   start=(kt == 0), stop=(kt == nkt - 1)),
                          se_mm)
                    amm.append(c.val(se_mm))
                if OPT_PREP_ACTRECIP:
                    s.wait_ge(se_mm, c.val(se_mm))
                    c.inc(s.activation(csrow[:], banks[3][0:1, 0:512], AF.Ln),
                          se_sq)
                    s.wait_ge(se_sq, c.val(se_sq))
                    c.inc(s.activation(csbf[0:1, 0:512], csrow[:], AF.Exp,
                                       scale=-1.0), se_sq)
                    t.wait_ge(se_sq, c.val(se_sq))
                else:
                    v.wait_ge(se_mm, c.val(se_mm))
                    c.inc(v.reciprocal(csrow[:], banks[3][0:1, 0:512]),
                          se_aux)
                    v.wait_ge(se_aux, c.val(se_aux))
                    c.inc(v.tensor_copy(csbf[0:1, 0:512], csrow[:]), se_aux)
                    t.wait_ge(se_aux, c.val(se_aux))
                t.wait_ge(se_ev, c.val(se_ev))
                c.inc(t.matmul(banks[4][:, 0:512], cst["onesrow_bf16"],
                               csbf[0:1, 0:512], start=True, stop=True), se_mm)
                v.wait_ge(se_mm, c.val(se_mm))
                c.inc(v.tensor_copy(rB[:], banks[4][:, 0:512]), se_ev)
                v.wait_ge(se_ev, c.val(se_ev))
                c.inc(v.tensor_tensor(tmpO[:], banks[2][:, 0:512], rB[:],
                                      ALU.mult), se_ev)
                zsl = zT[:, pair * N + qb * 512: pair * N + qb * 512 + 512]
                v.wait_ge(se_ev, c.val(se_ev))
                c.inc(v.scalar_tensor_tensor(
                    zsl, tmpO[:], 0.5, zsl, ALU.mult, ALU.add), se_ev)
                prev_ev = c.val(se_ev)
                prev_aux = c.val(se_aux)
                prev_sq = c.val(se_sq)
            if b == 1 and hh == 0:
                g.wait_ge(se_ev, c.val(se_ev))
                g.wait_ge(se_sq, c.val(se_sq))
                launch_a2a(c, 0, a2a_in, a2a_out, zT, sem)

    join_all(c, sem)
    launch_a2a(c, 1, a2a_in, a2a_out, zT, sem)
    pool = [xb, wvkb, wqb, wkb]
    wo_prefetched = False
    if "wo_dram" in ap:
        if "wo_pre_gate" in ap:
            g.wait_ge(ap["wo_pre_gate"][0], ap["wo_pre_gate"][1])
        wo_r = ap["wo_dram"].rearrange("hh j p n -> p (hh j) n")
        kc0 = 0
        for buf, sz in zip(pool, POOL_SIZES):
            nkc = sz // D
            c.inc(g.dma_start(
                out=buf[:, 0:nkc * D].rearrange("p (a n) -> p a n", a=nkc),
                in_=wo_r[:, kc0:kc0 + nkc, :]), se_wdma, 16)
            kc0 += nkc
        wo_prefetched = True
    g.wait_ge(se_cc, c.val(se_cc))
    inner.close()
    c._scopes.pop()
    return {"z0": a2a_out[0], "z1": a2a_out[1], "sem": sem,
            "pool": pool, "wo_prefetched": wo_prefetched}


def build_stage1_program():
    nc = bass.Bass("TRN2", target_bir_lowering=False)
    ap = declare_stage1_io(nc, external_z=True)
    with ExitStack() as es:
        c = Ctx(nc, es)
        es.enter_context(nc.Block())
        cst = setup_consts(c, nc)
        banks = [c.ps(f"s1pb{i}") for i in range(8)]
        r = emit_stage1(c, ap, cst, banks)
        sem = r["sem"]
        for hh in range(2):
            c.inc(c.g.dma_start(out=ap["z%d" % hh], in_=r["z%d" % hh]),
                  sem["dma"], 16)
        c.g.wait_ge(sem["dma"], c.val(sem["dma"]))
    return nc


def stage1_in_map(m):
    keys = ["xtt", "wq", "wk", "wvk", "id128", "mau", "my0", "minc", "mpt",
            "onescol", "onesrow_f32", "onesrow_bf16"]
    return {k: m[k] for k in keys}



def host_prep(inputs):
    x = np.ascontiguousarray(inputs["L0_attn_x"])
    wq = np.asarray(inputs["L0_attn_Wq"])
    wk = np.asarray(inputs["L0_attn_Wk"])
    wv = np.asarray(inputs["L0_attn_Wv"])
    wo = np.asarray(inputs["L0_attn_Wo"])
    w0 = np.asarray(inputs["L0_ffn_weight"])
    b0 = np.asarray(inputs["L0_ffn_bias"])
    w1 = np.asarray(inputs["L1_attn_W"])
    d1 = np.asarray(inputs["L1_attn_delta_W"])
    wg = np.asarray(inputs["L1_ffn_w_gate"])
    wu = np.asarray(inputs["L1_ffn_w_up"])
    wd = np.asarray(inputs["L1_ffn_w_down"])

    xtt = np.ascontiguousarray(
        x.reshape(B, 8, 512, 16, 128).transpose(0, 1, 4, 3, 2))
    wo_r = np.ascontiguousarray(wo.reshape(8, 2, 128, D).transpose(1, 0, 2, 3))
    w0t = np.ascontiguousarray(w0.T)
    b0r = np.ascontiguousarray(b0.reshape(16, 128).T)
    w1t = np.ascontiguousarray(w1.T)
    d1t = np.ascontiguousarray(d1.T)

    j = np.arange(8192)
    perm = (j % 2048) * 4 + j // 2048
    wgp = np.ascontiguousarray(
        wg[:, perm].reshape(16, 128, 64, 128).transpose(2, 0, 1, 3))
    wup = np.ascontiguousarray(
        wu[:, perm].reshape(16, 128, 64, 128).transpose(2, 0, 1, 3))
    wdp = np.ascontiguousarray(wd[perm, :].reshape(64, 128, D))
    wgup = np.ascontiguousarray(np.stack([wgp, wup], axis=1))

    id128 = np.eye(128).astype(ml_dtypes.bfloat16)
    r = np.arange(128)
    mau = (-(r[:, None] < r[None, :]).astype(np.float32)).astype(
        ml_dtypes.bfloat16)
    my0 = (-(r[:, None] > r[None, :]).astype(np.float32)).astype(
        ml_dtypes.bfloat16)
    minc = (r[:, None] <= r[None, :]).astype(np.float32).astype(
        ml_dtypes.bfloat16)
    mpt = np.zeros((4, 128, 512), np.float32)
    for i in range(4):
        for sgm in range(4):
            if sgm > i:
                mpt[i, :, sgm * 128:(sgm + 1) * 128] = 1.0
            elif sgm == i:
                mpt[i, :, sgm * 128:(sgm + 1) * 128] = (
                    r[:, None] <= r[None, :]).astype(np.float32)
    mpt = mpt.astype(ml_dtypes.bfloat16)
    onescol = np.ones((128, 1), ml_dtypes.bfloat16)
    onesrow_f32 = np.ones((1, 128), np.float32)
    onesrow_bf16 = np.ones((1, 128), ml_dtypes.bfloat16)

    fx = x.reshape(B * N, D)
    xt_full = np.ascontiguousarray(fx.T)

    in_maps = []
    for cix in range(NCORE):
        m = dict(
            xtt=xtt,
            xt_tok=np.ascontiguousarray(xt_full[:, cix * NTOK:(cix + 1) * NTOK]),
            wq=np.ascontiguousarray(wq[:, 2 * cix * 128:(2 * cix + 2) * 128]),
            wk=np.ascontiguousarray(wk[:, 2 * cix * 128:(2 * cix + 2) * 128]),
            wvk=np.ascontiguousarray(np.concatenate(
                [wv[:, 2 * cix * 128:(2 * cix + 2) * 128],
                 wk[:, 2 * cix * 128:(2 * cix + 2) * 128]], axis=1)),
            wo=wo_r, w0t=w0t, b0r=b0r, w1t=w1t, d1t=d1t,
            wgp=wgp, wup=wup, wgup=wgup, wdp=wdp,
            id128=id128, mau=mau, my0=my0, minc=minc, mpt=mpt,
            onescol=onescol, onesrow_f32=onesrow_f32,
            onesrow_bf16=onesrow_bf16,
        )
        in_maps.append(m)
    return in_maps



def setup_consts(c, nc):
    names = {}
    for nm, shape, dt in (("onescol", [128, 1], BF16),
                          ("onesrow_f32", [1, 128], F32),
                          ("onesrow_bf16", [1, 128], BF16)):
        names[nm] = nc.dram_tensor(nm, shape, dt, kind="ExternalInput").ap()
    onescol = c.sb("onescol_s", [128, 1], BF16)
    onesrow = c.sb("onesrow_s", [1, 128], F32)
    onesrowb = c.sb("onesrowb_s", [1, 128], BF16)
    cpool = c.sb("cpool", [128, 4], F32)
    cst = {"onescol": onescol[:], "onesrow_f32": onesrow[:],
           "onesrow_bf16": onesrowb[:],
           "eps_ln_1": cpool[0:1, 0:1], "one_p": cpool[:, 1:2]}
    c.v.memset(cpool[:, 0:1], EPS_LN)
    c.v.memset(cpool[:, 1:2], 1.0)
    sed = c.sem("cst_init")
    c.inc(c.y.dma_start(out=onescol[:], in_=names["onescol"]), sed, 16)
    c.inc(c.y.dma_start(out=onesrow[:], in_=names["onesrow_f32"]), sed, 16)
    c.inc(c.y.dma_start(out=onesrowb[:], in_=names["onesrow_bf16"]), sed, 16)
    for e in (c.t, c.v, c.s, c.g, c.y):
        e.wait_ge(sed, c.val(sed))
    return cst


def build_stage2_program(dbg_names=()):
    nc = bass.Bass("TRN2", target_bir_lowering=False)
    ap = declare_stage2_io(nc)
    z_src = {
        "z0": nc.dram_tensor("z0", [NTOK, NTOK], BF16, kind="ExternalInput").ap(),
        "z1": nc.dram_tensor("z1", [NTOK, NTOK], BF16, kind="ExternalInput").ap(),
    }
    dbg = {}
    for nm in dbg_names:
        dbg[nm] = nc.dram_tensor("dbg_" + nm, [128, DC * NTOK], F32,
                                 kind="ExternalOutput").ap()
    with ExitStack() as es:
        c = Ctx(nc, es)
        es.enter_context(nc.Block())
        cst = setup_consts(c, nc)
        banks = [c.ps(f"pb{i}") for i in range(8)]
        emit_stage2(c, ap, z_src, cst, banks, dbg=dbg)
    return nc


def build_fused_program(dbg_names=()):
    nc = bass.Bass("TRN2", target_bir_lowering=False)
    ap1 = declare_stage1_io(nc, external_z=False)
    ap2 = declare_stage2_io(nc)
    dbg = {}
    for nm in dbg_names:
        dbg[nm] = nc.dram_tensor("dbg_" + nm, [128, DC * NTOK], F32,
                                 kind="ExternalOutput").ap()
    with ExitStack() as es:
        c = Ctx(nc, es)
        es.enter_context(nc.Block())
        cst = setup_consts(c, nc)
        banks = [c.ps(f"pb{i}") for i in range(8)]
        pre, se_pre, pre_n = emit_prepass(c, ap2, nc)
        ap2["pre"] = pre
        ap1["wo_dram"] = pre["wo"]
        ap1["wo_pre_gate"] = (se_pre, pre_n)
        r = emit_stage1(c, ap1, cst, banks, dbg=dbg)
        s1_sem = r["sem"]
        join_all(c, s1_sem)
        for e in (c.t, c.v, c.s, c.g, c.y):
            e.wait_ge(se_pre, pre_n)
        emit_stage2(c, ap2, {"z0": r["z0"], "z1": r["z1"]}, cst, banks,
                    dbg=dbg, pool=r["pool"], wo_prefetched=r["wo_prefetched"])
    return nc


def stage2_in_map(m, z0, z1):
    keys = ["xt_tok", "wo", "w0t", "b0r", "w1t", "d1t", "wgp", "wup", "wgup",
            "wdp", "onescol", "onesrow_f32", "onesrow_bf16"]
    d = {k: m[k] for k in keys}
    d["z0"] = z0
    d["z1"] = z1
    return d


_PROG_CACHE = {}


def _get_prog(name, builder):
    if name not in _PROG_CACHE:
        _PROG_CACHE[name] = builder()
    return _PROG_CACHE[name]


def fused_in_map(m):
    keys = ["xtt", "wq", "wk", "wvk", "id128", "mau", "my0", "minc", "mpt",
            "onescol", "onesrow_f32", "onesrow_bf16",
            "xt_tok", "wo", "w0t", "b0r", "w1t", "d1t", "wgp", "wup", "wgup",
            "wdp"]
    return {k: m[k] for k in keys}


def kernel(**inputs):
    _ = inputs["L1_attn_target_W"]
    in_maps = host_prep(inputs)
    ncf = _get_prog("fused", build_fused_program)
    res = run_bass_kernel_spmd(
        ncf, [fused_in_map(m) for m in in_maps], core_ids=list(range(NCORE)))
    out = np.empty((B * N, D), np.float32)
    for cix in range(NCORE):
        o = np.asarray(res.results[cix]["out"])
        out[cix * NTOK:(cix + 1) * NTOK, :] = o.T
    return out.reshape(B, N, D)


def kernel_unfused(**inputs):
    in_maps = host_prep(inputs)

    nc1 = _get_prog("s1", build_stage1_program)
    res1 = run_bass_kernel_spmd(
        nc1, [stage1_in_map(m) for m in in_maps],
        core_ids=list(range(NCORE)))

    nc2 = _get_prog("s2", lambda: build_stage2_program(()))
    maps2 = []
    for cix in range(NCORE):
        maps2.append(stage2_in_map(
            in_maps[cix],
            np.asarray(res1.results[cix]["z0"]),
            np.asarray(res1.results[cix]["z1"])))
    res2 = run_bass_kernel_spmd(nc2, maps2, core_ids=list(range(NCORE)))

    out = np.empty((B * N, D), np.float32)
    for cix in range(NCORE):
        o = np.asarray(res2.results[cix]["out"])
        out[cix * NTOK:(cix + 1) * NTOK, :] = o.T
    return out.reshape(B, N, D)


# version 09
# speedup vs baseline: 1.1771x; 1.1771x over previous
import numpy as np
import ml_dtypes
from contextlib import ExitStack, contextmanager

import concourse.bass as bass
import concourse.mybir as mybir
from concourse.bass_utils import run_bass_kernel_spmd
from concourse.aot_env import allow_ici_all_to_all

F32 = mybir.dt.float32
BF16 = mybir.dt.bfloat16
FP8 = mybir.dt.float8e4
AF = mybir.ActivationFunctionType
ALU = mybir.AluOpType
AX = mybir.AxisListType

NCORE = 8
B, N, D = 2, 4096, 2048
NTOK = 1024
DC = 16
H = 16
DH = 128
EPS_LN = 1e-5
EPS_ATTN = 1e-6
GATE_NORM = 16.0


class Ctx:

    def __init__(self, nc, es):
        self.nc = nc
        self.es = es
        self.cnt = {}
        self.t = nc.tensor
        self.v = nc.vector
        self.s = nc.scalar
        self.g = nc.gpsimd
        self.y = nc.sync
        self._scopes = [es]

    def sem(self, name):
        h = self._scopes[0].enter_context(self.nc.semaphore(name))
        self.cnt[id(h)] = 0
        return h

    def inc(self, instr, sem, n=1):
        instr.then_inc(sem, n)
        self.cnt[id(sem)] += n
        return self.cnt[id(sem)]

    def val(self, sem):
        return self.cnt[id(sem)]

    _uid = 0

    def _un(self, name):
        Ctx._uid += 1
        return f"{name}_{Ctx._uid}"

    def sb(self, name, shape, dtype):
        return self._scopes[-1].enter_context(
            self.nc.sbuf_tensor(self._un(name), shape, dtype))

    def ps(self, name, shape=(128, 512), dtype=F32):
        return self._scopes[-1].enter_context(
            self.nc.psum_tensor(self._un(name), list(shape), dtype))

    @contextmanager
    def scope(self):
        es = ExitStack()
        self._scopes.append(es)
        try:
            yield
        finally:
            es.close()
            self._scopes.pop()


def emit_prepass(c, ap, nc):
    g = c.g
    se = c.sem("prepass")
    pre = {}
    pre["wo"] = nc.dram_tensor("wo_i", [2, 8, 128, D], BF16).ap()
    pre["wgup"] = nc.dram_tensor("wgup_i", [64, 2, 16, 128, 128], BF16).ap()
    pre["wdp"] = nc.dram_tensor("wdp_i", [64, 128, D], BF16).ap()
    pre["w1t"] = nc.dram_tensor("w1t_i", [D, D], BF16).ap()
    pre["d1t"] = nc.dram_tensor("d1t_i", [D, D], BF16).ap()
    pieces = []
    pieces.append(lambda: c.inc(
        g.dma_start(out=pre["wo"], in_=ap["wo"]), se, 16))
    pieces.append(lambda: c.inc(
        g.dma_start(out=pre["w1t"], in_=ap["w1t"]), se, 16))
    pieces.append(lambda: c.inc(
        g.dma_start(out=pre["d1t"], in_=ap["d1t"]), se, 16))
    for i in range(8):
        pieces.append(lambda i=i: c.inc(
            g.dma_start(out=pre["wgup"][i * 8:(i + 1) * 8],
                        in_=ap["wgup"][i * 8:(i + 1) * 8]), se, 16))
    for i in range(4):
        pieces.append(lambda i=i: c.inc(
            g.dma_start(out=pre["wdp"][i * 16:(i + 1) * 16],
                        in_=ap["wdp"][i * 16:(i + 1) * 16]), se, 16))
    return pre, se, 16 * len(pieces), pieces


def make_sems(c, tag):
    return {k: c.sem(tag + k) for k in
            ["cp", "sq", "mm", "ev", "dma", "wdma", "cc", "aux"]}


def join_all(c, sem):
    for e in (c.t, c.v, c.s, c.g, c.y):
        for sname in ("dma", "wdma", "mm", "ev", "cp", "sq", "aux", "cc"):
            if sname in sem:
                e.wait_ge(sem[sname], c.val(sem[sname]))



def emit_ln_fm(c, h, cst, ln_banks, sem, ntok=NTOK, hb_out=None):
    t, v, s = c.t, c.v, c.s
    se_cp, se_sq, se_mm, se_ev = sem["cp"], sem["sq"], sem["mm"], sem["ev"]
    ntb = ntok // 512
    bcast_bank = ln_banks[0]
    with c.scope():
        hb = c.sb("ln_hb", [128, 2 * ntok], BF16)
        sq = c.sb("ln_sq", [128, 2 * ntok], BF16)
        rb = c.sb("ln_rb", [128, ntok], F32)
        nbt = c.sb("ln_nb", [128, ntok], F32)
        row = c.sb("ln_row", [1, 4 * ntok], F32)

        mm_per_kc = 1
        for kc in range(DC):
            pp = kc % 2
            hk = h[:, kc * ntok:(kc + 1) * ntok]
            hbv = hb[:, pp * ntok:(pp + 1) * ntok]
            sqv = sq[:, pp * ntok:(pp + 1) * ntok]
            if kc >= 2:
                v.wait_ge(se_mm, c.val(se_mm) - mm_per_kc)
            c.inc(v.tensor_copy(hbv, hk), se_cp)
            if kc >= 2:
                s.wait_ge(se_mm, c.val(se_mm) - mm_per_kc)
            s.wait_ge(se_cp, c.val(se_cp))
            c.inc(s.activation(sqv, hbv, AF.Square), se_sq)
            t.wait_ge(se_sq, c.val(se_sq))
            mm = None
            for tb in range(ntb):
                t.matmul(ln_banks[tb][0:1, 0:512],
                         cst["onescol"], hbv[:, tb * 512:(tb + 1) * 512],
                         start=(kc == 0), stop=(kc == DC - 1))
                mm = t.matmul(ln_banks[2 + tb][0:1, 0:512],
                              cst["onescol"], sqv[:, tb * 512:(tb + 1) * 512],
                              start=(kc == 0), stop=(kc == DC - 1))
            c.inc(mm, se_mm)

        mrow = row[0:1, 0:ntok]
        vrow = row[0:1, ntok:2 * ntok]
        rrow = row[0:1, 2 * ntok:3 * ntok]
        nrow = row[0:1, 3 * ntok:4 * ntok]
        v.wait_ge(se_mm, c.val(se_mm))
        for tb in range(ntb):
            v.tensor_scalar(mrow[:, tb * 512:(tb + 1) * 512],
                            ln_banks[tb][0:1, 0:512], 1.0 / D, None, ALU.mult)
            v.tensor_scalar(vrow[:, tb * 512:(tb + 1) * 512],
                            ln_banks[2 + tb][0:1, 0:512], 1.0 / D, None,
                            ALU.mult)
        v.tensor_tensor(rrow, mrow, mrow, ALU.mult)
        c.inc(v.tensor_tensor(vrow, vrow, rrow, ALU.subtract), se_ev)
        s.wait_ge(se_ev, c.val(se_ev))
        s.activation(rrow, vrow, AF.Ln, bias=cst["eps_ln_1"])
        c.inc(s.activation(rrow, rrow, AF.Exp, scale=-0.5), se_sq)
        v.wait_ge(se_sq, c.val(se_sq))
        v.tensor_tensor(nrow, mrow, rrow, ALU.mult)
        c.inc(v.tensor_scalar(nrow, nrow, -1.0, None, ALU.mult), se_ev)

        t.wait_ge(se_ev, c.val(se_ev))
        for tb in range(ntb):
            for src, dst in ((rrow, rb), (nrow, nbt)):
                c.inc(t.matmul(bcast_bank[:, 0:512], cst["onesrow_f32"],
                               src[:, tb * 512:(tb + 1) * 512],
                               start=True, stop=True), se_mm)
                v.wait_ge(se_mm, c.val(se_mm))
                c.inc(v.tensor_copy(dst[:, tb * 512:(tb + 1) * 512],
                                    bcast_bank[:, 0:512]), se_ev)
                t.wait_ge(se_ev, c.val(se_ev))

        for kc in range(DC):
            hk = h[:, kc * ntok:(kc + 1) * ntok]
            v.tensor_tensor(hk, hk, rb[:, 0:ntok], ALU.mult)
            c.inc(v.tensor_tensor(hk, hk, nbt[:, 0:ntok], ALU.add), se_ev)
            if hb_out is not None:
                c.inc(v.tensor_copy(hb_out[:, kc * ntok:(kc + 1) * ntok], hk),
                      se_cp)



def emit_proj_fm(c, get_lhsT, get_rhs, evac, banks, sem, n_ms=16, ntb=2,
                 n_kc=16):
    t = c.t
    se_mm, se_ev = sem["mm"], sem["ev"]
    nb = len(banks)
    job = 0
    for ms in range(n_ms):
        for tb in range(ntb):
            bank = banks[job % nb]
            if job >= nb:
                t.wait_ge(se_ev, c.val(se_ev) - (nb - 1))
            i = None
            for kc in range(n_kc):
                i = t.matmul(bank[:, 0:512], get_lhsT(kc, ms), get_rhs(kc, tb),
                             start=(kc == 0), stop=(kc == n_kc - 1))
            c.inc(i, se_mm)
            evac(ms, tb, bank)
            job += 1
    t.wait_ge(se_ev, c.val(se_ev))


def partition_absmax(c, src_reducer, n_cols, sem, name, dst):
    nc = c.nc
    v, y = c.v, c.y
    se_dma, se_aux = sem["dma"], sem["aux"]
    with c.scope():
        amax = c.sb(name + "_amax", [128, n_cols], F32)
        row = c.sb(name + "_row", [1, 128 * n_cols], F32)
        bounce = nc.dram_tensor(name + "_bounce", [128, n_cols], F32).ap()
        src_reducer(amax)
        y.wait_ge(se_aux, c.val(se_aux))
        c.inc(y.dma_start(out=bounce, in_=amax[:]), se_dma, 16)
        y.wait_ge(se_dma, c.val(se_dma))
        c.inc(y.dma_start(
            out=row[:], in_=bounce.rearrange("p a -> (p a)")[None, :]),
            se_dma, 16)
        v.wait_ge(se_dma, c.val(se_dma))
        c.inc(v.tensor_reduce(dst, row[:], AX.X, ALU.max), se_aux)
        v.wait_ge(se_aux, c.val(se_aux))


def bcast_scalar(c, src11, dst_col, bank, cst, sem):
    t, v = c.t, c.v
    se_mm, se_ev = sem["mm"], sem["ev"]
    t.wait_ge(sem["aux"], c.val(sem["aux"]))
    t.wait_ge(se_ev, c.val(se_ev))
    c.inc(t.matmul(bank[:, 0:1], cst["onesrow_f32"], src11, start=True,
                   stop=True), se_mm)
    v.wait_ge(se_mm, c.val(se_mm))
    c.inc(v.tensor_copy(dst_col, bank[:, 0:1]), se_ev)


def dump_dbg(c, dbg, name, h, sem):
    if dbg and name in dbg:
        join_all(c, sem)
        c.inc(c.y.dma_start(out=dbg[name], in_=h[:]), sem["dma"], 16)
        c.y.wait_ge(sem["dma"], c.val(sem["dma"]))



def declare_stage2_io(nc):
    ap = {}

    def ein(name, shape, dt):
        ap[name] = nc.dram_tensor(name, shape, dt, kind="ExternalInput").ap()

    ein("xt_tok", [D, NTOK], F32)
    ein("wo", [2, 8, 128, D], F32)
    ein("w0t", [D, D], BF16)
    ein("b0r", [128, 16], BF16)
    ein("w1t", [D, D], F32)
    ein("d1t", [D, D], F32)
    ein("wgp", [64, 16, 128, 128], F32)
    ein("wup", [64, 16, 128, 128], F32)
    ein("wgup", [64, 2, 16, 128, 128], F32)
    ein("wdp", [64, 128, D], F32)
    ap["out"] = nc.dram_tensor("out", [D, NTOK], F32,
                               kind="ExternalOutput").ap()
    return ap


def emit_stage2(c, ap, z_src, cst, banks, dbg=None, pool=None,
                wo_prefetched=False):
    nc = c.nc
    t, v, s, g, y = c.t, c.v, c.s, c.g, c.y

    if pool is None:
        pool = [c.sb("pool%d" % i, [128, sz], BF16)
                for i, sz in enumerate(POOL_SIZES)]
    h = c.sb("h", [128, DC * NTOK], F32)
    sc = c.sb("sc", [128, 8], F32)
    s11 = c.sb("s11", [1, 16], F32)
    bias_bf = c.sb("bias_bf", [128, 16], BF16)
    bias_f = c.sb("bias_f", [128, 16], F32)

    pst, pbc = banks[6], banks[7]

    sem = make_sems(c, "s2")
    se_dma, se_wdma = sem["dma"], sem["wdma"]
    se_mm, se_ev, se_aux, se_cp = sem["mm"], sem["ev"], sem["aux"], sem["cp"]

    with c.scope():
        zt = c.sb("zt", [128, DC * NTOK], BF16)
        if not wo_prefetched:
            wo_r = ap["wo"].rearrange("hh j p n -> p (hh j) n")
            kc0 = 0
            for buf, sz in zip(pool, POOL_SIZES):
                nkc = sz // D
                c.inc(g.dma_start(
                    out=buf[:, 0:nkc * D].rearrange("p (a n) -> p a n", a=nkc),
                    in_=wo_r[:, kc0:kc0 + nkc, :]), se_wdma, 16)
                kc0 += nkc
        c.inc(y.dma_start(
            out=h[:].rearrange("p (a n) -> p a n", a=DC),
            in_=ap["xt_tok"].rearrange("(a p) n -> p a n", p=128)), se_dma, 16)
        for hh in range(2):
            c.inc(y.dma_start(
                out=zt[:].rearrange("p (a n) -> p a n", a=DC)[
                    :, hh * 8:(hh + 1) * 8, :],
                in_=z_src["z%d" % hh].rearrange("(j p) n -> p j n", p=128)),
                se_dma, 16)
        join_all(c, sem)

        def wo_lhsT(kc, ms):
            return pool_chunk(pool, kc, D)[:, ms * 128:(ms + 1) * 128]

        def wo_rhs(kc, tb):
            return zt[:, kc * NTOK + tb * 512: kc * NTOK + (tb + 1) * 512]

        def wo_evac(ms, tb, bank):
            v.wait_ge(se_mm, c.val(se_mm))
            dst = h[:, ms * NTOK + tb * 512: ms * NTOK + (tb + 1) * 512]
            c.inc(v.tensor_tensor(dst, dst, bank[:, 0:512], ALU.add), se_ev)

        emit_proj_fm(c, wo_lhsT, wo_rhs, wo_evac, banks[0:4], sem)
        join_all(c, sem)
        dump_dbg(c, dbg, "h1", h, sem)
        emit_ln_fm(c, h, cst, banks[4:8], sem)
        join_all(c, sem)
        dump_dbg(c, dbg, "h1n", h, sem)

    with c.scope():
        p2_inner = ExitStack()
        c._scopes.append(p2_inner)
        wq8 = c.sb("wq8", [128, DC * D], FP8)
        def red_h(amax):
            for kc in range(DC):
                c.inc(v.tensor_reduce(
                    amax[:, kc:kc + 1], h[:, kc * NTOK:(kc + 1) * NTOK],
                    AX.X, ALU.max, apply_absolute_value=True), se_aux)

        partition_absmax(c, red_h, DC, sem, "xs", s11[0:1, 0:1])
        c.inc(v.tensor_scalar(s11[0:1, 0:1], s11[0:1, 0:1], EPS_ATTN, None,
                              ALU.max), se_aux)
        armax_in = nc.dram_tensor("armax_in", [1, 8], F32).ap()
        armax_out = nc.dram_tensor("armax_out", [1, 8], F32).ap()
        v.memset(s11[0:1, 8:16], 0.0)
        v.wait_ge(se_aux, c.val(se_aux))
        c.inc(v.tensor_copy(s11[0:1, 8:9], s11[0:1, 0:1]), se_aux)
        g.wait_ge(se_aux, c.val(se_aux))
        c.inc(g.dma_start(out=armax_in, in_=s11[0:1, 8:16]), se_wdma, 16)
        g.wait_ge(se_wdma, c.val(se_wdma))
        c.inc(g.collective_compute(
            "AllReduce", ALU.max, replica_groups=[list(range(NCORE))],
            ins=[armax_in], outs=[armax_out]), sem["cc"])
        g.wait_ge(sem["cc"], c.val(sem["cc"]))
        c.inc(g.dma_start(out=s11[0:1, 1:2], in_=armax_out[0:1, 0:1]),
              se_wdma, 16)

        with c.scope():
            join_all(c, sem)
            pool_dma_loads(c, y, pool, ap["w0t"], DC, D, se_dma, False)
            c.inc(y.dma_start(out=bias_bf[:], in_=ap["b0r"]), se_dma, 16)
            v.wait_ge(se_dma, c.val(se_dma))
            c.inc(v.tensor_copy(bias_f[:], bias_bf[:]), se_aux)

            def red_w(amax):
                for kc in range(DC):
                    c.inc(v.tensor_reduce(
                        amax[:, kc:kc + 1], pool_chunk(pool, kc, D),
                        AX.X, ALU.max, apply_absolute_value=True), se_aux)

            partition_absmax(c, red_w, DC, sem, "ws", s11[0:1, 2:3])
            c.inc(v.tensor_scalar(s11[0:1, 2:3], s11[0:1, 2:3], EPS_ATTN,
                                  None, ALU.max), se_aux)
            v.wait_ge(se_aux, c.val(se_aux))
            c.inc(v.reciprocal(s11[0:1, 3:4], s11[0:1, 2:3]), se_aux)
            bcast_scalar(c, s11[0:1, 3:4], sc[:, 1:2], pbc, cst, sem)
            v.wait_ge(se_ev, c.val(se_ev))
            for kc in range(DC):
                c.inc(v.tensor_scalar(wq8[:, kc * D:(kc + 1) * D],
                                      pool_chunk(pool, kc, D),
                                      sc[:, 1:2], None, ALU.mult), se_aux)

        v.wait_ge(se_wdma, c.val(se_wdma))
        c.inc(v.reciprocal(s11[0:1, 4:5], s11[0:1, 1:2]), se_aux)
        v.wait_ge(se_aux, c.val(se_aux))
        c.inc(v.tensor_tensor(s11[0:1, 5:6], s11[0:1, 1:2], s11[0:1, 2:3],
                              ALU.mult), se_aux)
        bcast_scalar(c, s11[0:1, 4:5], sc[:, 0:1], pbc, cst, sem)
        bcast_scalar(c, s11[0:1, 5:6], sc[:, 2:3], pbc, cst, sem)
        xq8 = c.sb("xq8", [128, DC * NTOK], FP8)
        v.wait_ge(se_ev, c.val(se_ev))
        for kc in range(DC):
            c.inc(v.tensor_scalar(xq8[:, kc * NTOK:(kc + 1) * NTOK],
                                  h[:, kc * NTOK:(kc + 1) * NTOK],
                                  sc[:, 0:1], None, ALU.mult), se_aux)
        join_all(c, sem)

        ybf = c.sb("ybf", [128, 512], BF16)
        ybf2 = c.sb("ybf2", [128, 512], BF16)

        def f0_lhsT(kc, ms):
            return wq8[:, kc * D + ms * 128: kc * D + (ms + 1) * 128]

        def f0_rhs(kc, tb):
            return xq8[:, kc * NTOK + tb * 512: kc * NTOK + (tb + 1) * 512]

        def f0_evac(ms, tb, bank):
            v.wait_ge(se_mm, c.val(se_mm))
            v.tensor_scalar(ybf[:], bank[:, 0:512], sc[:, 2:3], None, ALU.mult)
            v.tensor_scalar(ybf2[:], ybf[:], bias_f[:, ms:ms + 1], None,
                            ALU.add)
            dst = h[:, ms * NTOK + tb * 512: ms * NTOK + (tb + 1) * 512]
            c.inc(v.tensor_tensor(dst, dst, ybf2[:], ALU.add), se_ev)

        emit_proj_fm(c, f0_lhsT, f0_rhs, f0_evac, banks[0:4], sem)
        join_all(c, sem)
        p2_inner.close()
        c._scopes.pop()
        dump_dbg(c, dbg, "h2", h, sem)
        emit_ln_fm(c, h, cst, banks[4:8], sem)
        join_all(c, sem)
        dump_dbg(c, dbg, "h2n", h, sem)

    with c.scope():
        h2nb = c.sb("h2nb", [128, DC * NTOK], BF16)
        for kc in range(DC):
            c.inc(v.tensor_copy(h2nb[:, kc * NTOK:(kc + 1) * NTOK],
                                h[:, kc * NTOK:(kc + 1) * NTOK]), se_cp)
        join_all(c, sem)
        wtmp = c.sb("wtmp", [128, 2 * D], BF16)
        w1src = ap["pre"]["w1t"] if "pre" in ap else None
        d1src = ap["pre"]["d1t"] if "pre" in ap else None
        for kc in range(DC):
            pp = kc % 2
            if kc >= 2:
                y.wait_ge(se_aux, c.val(se_aux) - 1)
            if w1src is not None:
                c.inc(y.dma_start(out=wtmp[:, pp * D:pp * D + D],
                                  in_=w1src[kc * 128:(kc + 1) * 128, :]),
                      se_wdma, 16)
                c.inc(y.dma_start(out=pool_chunk(pool, kc, D),
                                  in_=d1src[kc * 128:(kc + 1) * 128, :]),
                      se_wdma, 16)
            else:
                c.inc(g.dma_start(out=wtmp[:, pp * D:pp * D + D],
                                  in_=ap["w1t"][kc * 128:(kc + 1) * 128, :]),
                      se_wdma, 16)
                c.inc(g.dma_start(out=pool_chunk(pool, kc, D),
                                  in_=ap["d1t"][kc * 128:(kc + 1) * 128, :]),
                      se_wdma, 16)
            v.wait_ge(se_wdma, c.val(se_wdma))
            c.inc(v.tensor_tensor(pool_chunk(pool, kc, D),
                                  pool_chunk(pool, kc, D),
                                  wtmp[:, pp * D:pp * D + D], ALU.add), se_aux)
        join_all(c, sem)

        def l1_lhsT(kc, ms):
            return pool_chunk(pool, kc, D)[:, ms * 128:(ms + 1) * 128]

        def l1_rhs(kc, tb):
            return h2nb[:, kc * NTOK + tb * 512: kc * NTOK + (tb + 1) * 512]

        def l1_evac(ms, tb, bank):
            v.wait_ge(se_mm, c.val(se_mm))
            dst = h[:, ms * NTOK + tb * 512: ms * NTOK + (tb + 1) * 512]
            c.inc(v.tensor_tensor(dst, dst, bank[:, 0:512], ALU.add), se_ev)

        emit_proj_fm(c, l1_lhsT, l1_rhs, l1_evac, banks[0:4], sem)
        join_all(c, sem)
        dump_dbg(c, dbg, "h3", h, sem)

    with c.scope():
        h3nb = c.sb("h3nb", [128, DC * NTOK], BF16)
        emit_ln_fm(c, h, cst, banks[4:8], sem, hb_out=h3nb)
        join_all(c, sem)
        dump_dbg(c, dbg, "h3n", h, sem)

        p4_inner = ExitStack()
        self_scopes = c._scopes
        self_scopes.append(p4_inner)
        join_all(c, sem)
        use_pre = "pre" in ap
        hm4 = c.sb("hm4", [128, 4 * 512], F32)
        ab4 = c.sb("ab4", [128, 4 * 512], F32)
        tt1 = c.sb("tt1", [128, 512], F32)
        tt2 = c.sb("tt2", [128, 512], F32)
        tt3 = c.sb("tt3", [128, 512], F32)
        sg = c.sb("sg", [128, 2 * 512], F32)

        for th in range(2):
            tok0 = th * 512
            seq = [(gb, mm) for gb in range(16) for mm in range(4)]

            es_g = ExitStack()
            self_scopes.append(es_g)
            wgu = c.sb("wgu", [128, 2 * 2 * D], BF16)

            DEPTH = 2

            def p4w_load(k):
                gb_, mm_ = seq[k]
                m_ = mm_ * 16 + gb_
                sl = (k % DEPTH) * 2 * D
                if use_pre:
                    c.inc(g.dma_start(
                        out=wgu[:, sl:sl + 2 * D].rearrange(
                            "p (b a n) -> p b a n", b=2, a=16),
                        in_=ap["pre"]["wgup"][m_].rearrange(
                            "b a p n -> p b a n")), se_wdma, 16)
                else:
                    c.inc(g.dma_start(
                        out=wgu[:, sl:sl + D].rearrange(
                            "p (a n) -> p a n", a=16),
                        in_=ap["wgp"][m_].rearrange("a p n -> p a n")),
                        se_wdma, 16)
                    c.inc(g.dma_start(
                        out=wgu[:, sl + D:sl + 2 * D].rearrange(
                            "p (a n) -> p a n", a=16),
                        in_=ap["wup"][m_].rearrange("a p n -> p a n")),
                        se_wdma, 16)
                return c.val(se_wdma)

            wsnap = [p4w_load(kk) for kk in range(DEPTH)]
            esnap, qsnap, msnap = [], [], []
            for k, (gb, mm) in enumerate(seq):
                m = mm * 16 + gb
                sl = (k % DEPTH) * 2 * D
                t.wait_ge(se_wdma, wsnap[k])
                if k >= 2:
                    t.wait_ge(se_ev, esnap[k - 2])
                    t.wait_ge(sem["sq"], qsnap[k - 2])
                pg = banks[2 * (k % 2)]
                pu = banks[2 * (k % 2) + 1]
                i = None
                for kc in range(DC):
                    t.matmul(
                        pg[:, 0:512],
                        wgu[:, sl + kc * 128: sl + (kc + 1) * 128],
                        h3nb[:, kc * NTOK + tok0: kc * NTOK + tok0 + 512],
                        start=(kc == 0), stop=(kc == DC - 1))
                    i = t.matmul(
                        pu[:, 0:512],
                        wgu[:, sl + D + kc * 128: sl + D + (kc + 1) * 128],
                        h3nb[:, kc * NTOK + tok0: kc * NTOK + tok0 + 512],
                        start=(kc == 0), stop=(kc == DC - 1))
                c.inc(i, se_mm)
                msnap.append(c.val(se_mm))
                if k + DEPTH < len(seq):
                    g.wait_ge(se_mm, msnap[k])
                    wsnap.append(p4w_load(k + DEPTH))
                s.wait_ge(se_mm, c.val(se_mm))
                c.inc(s.activation(sg[:, (k % 2) * 512:(k % 2) * 512 + 512],
                                   pg[:, 0:512], AF.Silu), sem["sq"])
                qsnap.append(c.val(sem["sq"]))
                v.wait_ge(se_mm, c.val(se_mm))
                v.wait_ge(sem["sq"], c.val(sem["sq"]))
                c.inc(v.tensor_tensor(hm4[:, mm * 512:(mm + 1) * 512],
                                      sg[:, (k % 2) * 512:(k % 2) * 512 + 512],
                                      pu[:, 0:512], ALU.mult), se_ev)
                esnap.append(c.val(se_ev))
                if mm == 3:
                    s.wait_ge(se_ev, c.val(se_ev))
                    for mm2 in range(4):
                        c.inc(s.activation(ab4[:, mm2 * 512:(mm2 + 1) * 512],
                                           hm4[:, mm2 * 512:(mm2 + 1) * 512],
                                           AF.Abs), sem["sq"])
                    v.wait_ge(sem["sq"], c.val(sem["sq"]))
                    A = [ab4[:, x * 512:(x + 1) * 512] for x in range(4)]
                    Hm = [hm4[:, x * 512:(x + 1) * 512] for x in range(4)]
                    v.tensor_tensor(tt1[:], A[0], A[1], ALU.max)
                    v.tensor_tensor(tt2[:], A[2], A[3], ALU.max)
                    v.tensor_tensor(tt1[:], tt1[:], tt2[:], ALU.min)
                    v.tensor_tensor(tt2[:], A[0], A[1], ALU.min)
                    v.tensor_tensor(tt3[:], A[2], A[3], ALU.min)
                    v.tensor_tensor(tt2[:], tt2[:], tt3[:], ALU.max)
                    v.tensor_tensor(tt1[:], tt1[:], tt2[:], ALU.max)
                    for mm2 in range(4):
                        m2 = mm2 * 16 + gb
                        v.tensor_tensor(tt2[:], A[mm2], tt1[:], ALU.is_ge)
                        c.inc(v.tensor_tensor(
                            pool_chunk(pool, m2, 512), Hm[mm2], tt2[:],
                            ALU.mult), se_ev)
                    esnap[-1] = c.val(se_ev)
                    qsnap[-1] = c.val(sem["sq"])
            join_all(c, sem)
            es_g.close()
            self_scopes.pop()
            es_d = ExitStack()
            self_scopes.append(es_d)
            wdt = c.sb("wdt", [128, 2 * D], BF16)

            wdsrc = ap["pre"]["wdp"] if use_pre else None
            for msg in range(2):
                def p4d_load(mp_):
                    sl_ = (mp_ % 2) * D
                    if wdsrc is not None:
                        c.inc(g.dma_start(
                            out=wdt[:, sl_:sl_ + D].rearrange(
                                "p (m n) -> p m n", m=2),
                            in_=wdsrc[2 * mp_:2 * mp_ + 2, :,
                                      msg * NTOK:(msg + 1) * NTOK].rearrange(
                                "m p n -> p m n")), se_wdma, 16)
                    else:
                        c.inc(g.dma_start(
                            out=wdt[:, sl_:sl_ + D].rearrange(
                                "p (m n) -> p m n", m=2),
                            in_=ap["wdp"][2 * mp_:2 * mp_ + 2, :,
                                          msg * NTOK:(msg + 1) * NTOK].rearrange(
                                "m p n -> p m n")), se_wdma, 16)
                    return c.val(se_wdma)

                dw = [p4d_load(0), p4d_load(1)]
                dm = []
                for mp in range(32):
                    sl_ = (mp % 2) * D
                    t.wait_ge(se_wdma, dw[mp])
                    i = None
                    for m2 in range(2):
                        m = 2 * mp + m2
                        for ms8 in range(8):
                            i = t.matmul(
                                banks[ms8][:, 0:512],
                                wdt[:, sl_ + m2 * NTOK + ms8 * 128:
                                    sl_ + m2 * NTOK + (ms8 + 1) * 128],
                                pool_chunk(pool, m, 512),
                                start=(m == 0), stop=(m == 63))
                    c.inc(i, se_mm)
                    dm.append(c.val(se_mm))
                    if mp + 2 < 32:
                        g.wait_ge(se_mm, dm[mp])
                        dw.append(p4d_load(mp + 2))
                v.wait_ge(se_mm, c.val(se_mm))
                for ms8 in range(8):
                    msx = msg * 8 + ms8
                    dst = h[:, msx * NTOK + tok0: msx * NTOK + tok0 + 512]
                    c.inc(v.tensor_tensor(dst, dst, banks[ms8][:, 0:512],
                                          ALU.add), se_ev)
                join_all(c, sem)
            es_d.close()
            self_scopes.pop()
        p4_inner.close()
        self_scopes.pop()
        dump_dbg(c, dbg, "h4", h, sem)
        emit_ln_fm(c, h, cst, banks[4:8], sem)
        join_all(c, sem)

    c.inc(y.dma_start(
        out=ap["out"].rearrange("(a p) n -> p a n", p=128),
        in_=h[:].rearrange("p (a n) -> p a n", a=DC)), sem["dma"], 16)
    y.wait_ge(sem["dma"], c.val(sem["dma"]))


def se_sq_g(sem):
    return sem["sq"]



def launch_a2a(c, hh, a2a_in, a2a_out, zT, sem):
    g = c.g
    for b in range(2):
        pair = b * 2 + hh
        c.inc(g.dma_start(
            out=a2a_in[hh][b * 512:(b + 1) * 512, :].rearrange(
                "(r p) n -> p r n", p=128),
            in_=zT[:, pair * N:(pair + 1) * N].rearrange(
                "p (m n) -> p m n", m=4)), sem["dma"], 16)
    g.wait_ge(sem["dma"], c.val(sem["dma"]))
    with allow_ici_all_to_all():
        c.inc(g.collective_compute(
            "AllToAll", ALU.bypass,
            replica_groups=[list(range(NCORE))],
            ins=[a2a_in[hh]], outs=[a2a_out[hh]]), sem["cc"])


def declare_stage1_io(nc, external_z):
    ap = {}

    def ein(name, shape, dt):
        ap[name] = nc.dram_tensor(name, shape, dt, kind="ExternalInput").ap()

    ein("xtt", [B, 8, 128, 16, 512], F32)
    ein("wq", [D, 256], F32)
    ein("wk", [D, 256], F32)
    ein("wvk", [D, 512], F32)
    ein("id128", [128, 128], BF16)
    ein("mau", [128, 128], BF16)
    ein("my0", [128, 128], BF16)
    ein("minc", [128, 128], BF16)
    ein("mpt", [4, 128, 512], BF16)
    if external_z:
        for hh in range(2):
            ap["z%d" % hh] = nc.dram_tensor(
                "z%d" % hh, [NTOK, NTOK], BF16, kind="ExternalOutput").ap()
    return ap


SQ_DH = 0.08838834764831845
DEBUG_JOIN_CHUNK = 0
OPT_CHAIN_ACT = False
OPT_PREP_ACTRECIP = True
OPT_PROJ_ALT = True

POOL_SIZES = [16384, 8192, 4096, 4096]


def pool_chunk(pool, idx, width):
    o = idx * width
    for buf, sz in zip(pool, POOL_SIZES):
        if o < sz:
            assert o + width <= sz
            return buf[:, o:o + width]
        o -= sz
    raise IndexError


def pool_dma_loads(c, eng, pool, in_2d, n_chunks, chunk_elems, sem, casted):
    src = in_2d.rearrange("(a p) n -> p a n", p=128)
    kc0 = 0
    for buf, sz in zip(pool, POOL_SIZES):
        nkc = sz // chunk_elems
        nkc = min(nkc, n_chunks - kc0)
        if nkc <= 0:
            break
        c.inc(eng.dma_start(
            out=buf[:, 0:nkc * chunk_elems].rearrange(
                "p (a n) -> p a n", a=nkc),
            in_=src[:, kc0:kc0 + nkc, :]), sem, 16)
        kc0 += nkc


def emit_stage1(c, ap, cst, banks, dbg=None):
    nc = c.nc
    t, v, s, g, y = c.t, c.v, c.s, c.g, c.y
    sem = make_sems(c, "s1")
    se_mm, se_ev, se_aux = sem["mm"], sem["ev"], sem["aux"]
    se_cp, se_sq, se_dma, se_wdma = sem["cp"], sem["sq"], sem["dma"], sem["wdma"]
    se_xb = c.sem("s1xb")
    se_cc = sem["cc"]

    a2a_in = [nc.dram_tensor("a2a_in%d" % hh, [8 * 128, NTOK], BF16).ap()
              for hh in range(2)]
    a2a_out = [nc.dram_tensor("a2a_out%d" % hh, [NTOK, NTOK], BF16).ap()
               for hh in range(2)]

    wqb = c.sb("wqb", [128, 16 * 256], BF16)
    wkb = c.sb("wkb", [128, 16 * 256], BF16)
    wvkb = c.sb("wvkb", [128, 16 * 512], BF16)
    xb = c.sb("xb", [128, 2 * 16 * 512], BF16)
    inner = ExitStack()
    c._scopes.append(inner)
    qTf = c.sb("qTf", [128, 2 * N], BF16)
    kTf = c.sb("kTf", [128, 2 * N], BF16)
    vtm = c.sb("vtm", [128, 64 * 256], BF16)
    zT = c.sb("zTs", [128, 4 * N], BF16)
    id128b = c.sb("id128b", [128, 128], BF16)
    maub = c.sb("maub", [128, 128], BF16)
    my0b = c.sb("my0b", [128, 128], BF16)
    mincb = c.sb("mincb", [128, 128], BF16)
    mptb = c.sb("mptb", [128, 4 * 512], BF16)
    ktm4 = c.sb("ktm4", [128, 2 * 4 * 256], BF16)

    EB = c.sb("EB", [128, 2 * 512], BF16)
    QKl = c.sb("QKl", [128, 2 * 512], BF16)
    csrow = c.sb("csrow", [1, 512], F32)
    csbf = c.sb("csbf", [1, 2 * 512], BF16)
    Ek = c.sb("Ek", [128, 2 * 256], BF16)
    sum2 = c.sb("sum2", [128, 2], F32)
    rcp2 = c.sb("rcp2", [128, 2], F32)
    klt_tm = c.sb("klt_tm", [128, 2 * 256], BF16)
    tA = c.sb("tA", [128, 256], F32)
    tBv = c.sb("tB", [128, 2 * 256], F32)
    dtok = c.sb("dtok", [128, 256], F32)
    dTf = c.sb("dTf", [128, 256], F32)
    vp_tm = c.sb("vp_tm", [128, 2 * 256], BF16)
    kb_tm = c.sb("kb_tm", [128, 2 * 256], BF16)
    kbT = c.sb("kbT", [128, 2 * 256], BF16)
    Au = [c.sb("Au", [128, 128], BF16) for _ in range(2)]
    Ybuf = [c.sb("Ybuf", [128, 128], BF16) for _ in range(2)]
    chainB = [c.sb("chainB", [128, 256], BF16) for _ in range(2)]
    Tt = [c.sb("Tt", [128, 128], BF16) for _ in range(2)]
    nwT = [c.sb("nwT", [128, 128], BF16) for _ in range(2)]
    up_tm = [c.sb("up_tm", [128, 128], BF16) for _ in range(2)]
    atT = [c.sb("atT", [128, 128], BF16) for _ in range(2)]
    S_sb = c.sb("S_sb", [128, 2 * 128], BF16)
    pt = c.sb("ptb", [128, 2 * 512], BF16)
    rB = c.sb("rBb", [128, 512], BF16)
    tmpO = c.sb("tmpO", [128, 512], BF16)
    mmsnap = {}
    s_free = [0, 0]
    chain_free = [0, 0]
    z_free = [0, 0]

    c.inc(y.dma_start(out=id128b[:], in_=ap["id128"]), se_dma, 16)
    c.inc(y.dma_start(out=maub[:], in_=ap["mau"]), se_dma, 16)
    c.inc(y.dma_start(out=my0b[:], in_=ap["my0"]), se_dma, 16)
    c.inc(y.dma_start(out=mincb[:], in_=ap["minc"]), se_dma, 16)
    c.inc(y.dma_start(out=mptb[:].rearrange("p (i n) -> p i n", i=4),
                      in_=ap["mpt"].rearrange("i p n -> p i n")), se_dma, 16)
    c.inc(g.dma_start(out=wqb[:].rearrange("p (a n) -> p a n", a=16),
                      in_=ap["wq"].rearrange("(a p) n -> p a n", p=128)),
          se_wdma, 16)
    c.inc(g.dma_start(out=wkb[:].rearrange("p (a n) -> p a n", a=16),
                      in_=ap["wk"].rearrange("(a p) n -> p a n", p=128)),
          se_wdma, 16)
    c.inc(g.dma_start(out=wvkb[:].rearrange("p (a n) -> p a n", a=16),
                      in_=ap["wvk"].rearrange("(a p) n -> p a n", p=128)),
          se_wdma, 16)
    join_all(c, sem)

    S_banks = (banks[6], banks[7])

    def xb_load(b_, tg_, slot):
        c.inc(g.dma_start(
            out=xb[:, slot * 8192:(slot + 1) * 8192].rearrange(
                "p (a n) -> p a n", a=16),
            in_=ap["xtt"][b_, tg_]), se_wdma, 16)

    xb_load(0, 0, 0)

    phase_floor = 0
    tile_mm_done = {}
    for b in range(2):
        phase_floor = c.val(se_mm)
        for tg in range(8):
            tile_no = b * 8 + tg
            pp = tile_no % 2
            xv = xb[:, pp * 8192:(pp + 1) * 8192]
            if tile_no + 1 < 16:
                nb_, ntg_ = divmod(tile_no + 1, 8)
                if tile_no >= 1:
                    g.wait_ge(se_mm, tile_mm_done[tile_no - 1])
                xb_load(nb_, ntg_, (tile_no + 1) % 2)
            pieces = ap.get("prepass_pieces")
            if pieces and tile_no < len(pieces):
                pieces[tile_no]()
            t.wait_ge(se_wdma, 48 + 16 * (tile_no + 1))

            if DEBUG_JOIN_CHUNK == 3:
                join_all(c, sem)
            t.wait_ge(se_aux, c.val(se_aux))
            t.wait_ge(se_ev, c.val(se_ev))
            t.wait_ge(se_sq, c.val(se_sq))
            pj_ev = []
            job = 0
            for j, wsb in enumerate((wqb, wkb)):
                for hh in range(2):
                    bank = banks[4 + job % 2]
                    if job >= 2:
                        t.wait_ge(se_ev, pj_ev[job - 2])
                    i = None
                    for kc in range(DC):
                        i = t.matmul(
                            bank[:, 0:512],
                            wsb[:, kc * 256 + hh * 128: kc * 256 + (hh + 1) * 128],
                            xv[:, kc * 512:(kc + 1) * 512],
                            start=(kc == 0), stop=(kc == DC - 1))
                    c.inc(i, se_mm)
                    dstf = qTf if j == 0 else kTf
                    dv_ = dstf[:, hh * N + tg * 512: hh * N + tg * 512 + 512]
                    if job % 2 == 0 or not OPT_PROJ_ALT:
                        v.wait_ge(se_mm, c.val(se_mm))
                        c.inc(v.tensor_copy(dv_, bank[:, 0:512]), se_ev)
                    else:
                        s.wait_ge(se_mm, c.val(se_mm))
                        c.inc(s.activation(dv_, bank[:, 0:512], AF.Copy),
                              se_ev)
                    pj_ev.append(c.val(se_ev))
                    job += 1
            for tt in range(4):
                bank = banks[4 + job % 2]
                t.wait_ge(se_ev, pj_ev[job - 2])
                i = None
                for kc in range(DC):
                    i = t.matmul(
                        bank[:, 0:512],
                        xv[:, kc * 512 + tt * 128: kc * 512 + (tt + 1) * 128],
                        wvkb[:, kc * 512:(kc + 1) * 512],
                        start=(kc == 0), stop=(kc == DC - 1))
                c.inc(i, se_mm)
                v.wait_ge(se_mm, c.val(se_mm))
                c.inc(v.tensor_copy(
                    vtm[:, ((b * 8 + tg) * 4 + tt) * 256:
                        ((b * 8 + tg) * 4 + tt) * 256 + 256],
                    bank[:, 0:256]), se_ev)
                c.inc(v.tensor_copy(
                    ktm4[:, (tg % 2) * 1024 + tt * 256:
                         (tg % 2) * 1024 + (tt + 1) * 256],
                    bank[:, 256:512]), se_ev)
                pj_ev.append(c.val(se_ev))
                job += 1
            tile_mm_done[tile_no] = c.val(se_mm)
            proj_done = c.val(se_ev)

            for tt in range(4):
                ch = tg * 4 + tt
                gch = b * 32 + ch
                par = gch % 2
                kraw = ktm4[:, (tg % 2) * 1024 + tt * 256:
                            (tg % 2) * 1024 + (tt + 1) * 256]
                vsl = vtm[:, ((b * 8 + tg) * 4 + tt) * 256:
                          ((b * 8 + tg) * 4 + tt) * 256 + 256]
                EBp = EB[:, par * 512:(par + 1) * 512]
                QKp = QKl[:, par * 512:(par + 1) * 512]
                kbTp = kbT[:, par * 256:(par + 1) * 256]
                Ekp = Ek[:, par * 256:(par + 1) * 256]
                kltp = klt_tm[:, par * 256:(par + 1) * 256]
                vpp = vp_tm[:, par * 256:(par + 1) * 256]
                kbp = kb_tm[:, par * 256:(par + 1) * 256]
                tBp = tBv[:, par * 256:(par + 1) * 256]
                csbp = csbf[:, par * 512:(par + 1) * 512]

                lag = max(mmsnap.get(gch - 2, 0), phase_floor)
                s.wait_ge(se_mm, lag)
                v.wait_ge(se_mm, lag)
                s.wait_ge(se_ev, proj_done)
                v.wait_ge(se_ev, proj_done)

                c.inc(s.activation(Ekp, kraw, AF.Exp), se_sq)
                v.wait_ge(se_sq, c.val(se_sq))
                c.inc(v.tensor_reduce(
                    sum2[:], Ekp.rearrange("p (i n) -> p i n", i=2),
                    AX.X, ALU.add), se_aux)
                v.wait_ge(se_aux, c.val(se_aux))
                c.inc(v.reciprocal(rcp2[:], sum2[:]), se_aux)
                v.wait_ge(se_aux, c.val(se_aux))
                for hh in range(2):
                    v.tensor_scalar(
                        kltp[:, hh * 128:(hh + 1) * 128],
                        Ekp[:, hh * 128:(hh + 1) * 128],
                        rcp2[:, hh:hh + 1], 1.0 - EPS_ATTN, ALU.mult, ALU.min)
                    c.inc(v.tensor_scalar(
                        kltp[:, hh * 128:(hh + 1) * 128],
                        kltp[:, hh * 128:(hh + 1) * 128],
                        EPS_ATTN, None, ALU.max), se_aux)
                c.inc(s.activation(tA[:], kraw, AF.Abs), se_sq)
                s.wait_ge(se_sq, c.val(se_sq))
                c.inc(s.activation(tBp, tA[:], AF.Exp, scale=-1.0), se_sq)
                s.wait_ge(se_sq, c.val(se_sq))
                c.inc(s.activation(tBp, tBp, AF.Ln, bias=cst["one_p"]), se_sq)
                v.tensor_scalar(dtok[:], kraw, 0.0, None, ALU.min)
                v.wait_ge(se_sq, c.val(se_sq))
                v.tensor_tensor(dtok[:], dtok[:], tBp, ALU.subtract)
                c.inc(v.tensor_scalar(dtok[:], dtok[:], -256.0, 0.0,
                                      ALU.max, ALU.min), se_aux)
                c.inc(v.scalar_tensor_tensor(
                    vpp, dtok[:], 1.0 / GATE_NORM, vsl,
                    ALU.mult, ALU.mult), se_aux)
                c.inc(v.scalar_tensor_tensor(
                    kbp, dtok[:], 1.0 / GATE_NORM, kltp,
                    ALU.mult, ALU.mult), se_aux)

                for hh in range(2):
                    c.inc(s.activation(
                        EBp[:, hh * 128:(hh + 1) * 128],
                        qTf[:, hh * N + ch * 128: hh * N + (ch + 1) * 128],
                        AF.Exp), se_sq)
                    c.inc(s.activation(
                        EBp[:, 256 + hh * 128: 256 + (hh + 1) * 128],
                        kTf[:, hh * N + ch * 128: hh * N + (ch + 1) * 128],
                        AF.Exp), se_sq)
                t.wait_ge(se_sq, c.val(se_sq))
                t.wait_ge(se_aux, c.val(se_aux))
                t.wait_ge(se_ev, proj_done)
                c.inc(t.matmul(banks[4][0:1, 0:512], cst["onescol"], EBp,
                               start=True, stop=True), se_mm)
                if OPT_PREP_ACTRECIP:
                    s.wait_ge(se_mm, c.val(se_mm))
                    c.inc(s.activation(csrow[:], banks[4][0:1, 0:512], AF.Ln),
                          se_sq)
                    s.wait_ge(se_sq, c.val(se_sq))
                    c.inc(s.activation(csbp, csrow[:], AF.Exp, scale=-1.0),
                          se_sq)
                    t.wait_ge(se_sq, c.val(se_sq))
                else:
                    v.wait_ge(se_mm, c.val(se_mm))
                    c.inc(v.reciprocal(csrow[:], banks[4][0:1, 0:512]),
                          se_aux)
                    v.wait_ge(se_aux, c.val(se_aux))
                    c.inc(v.tensor_copy(csbp, csrow[:]), se_aux)
                    t.wait_ge(se_aux, c.val(se_aux))
                c.inc(t.matmul(banks[5][:, 0:512], cst["onesrow_bf16"],
                               csbp, start=True, stop=True), se_mm)
                v.wait_ge(se_mm, c.val(se_mm))
                v.tensor_tensor(QKp, EBp, banks[5][:, 0:512], ALU.mult)
                v.tensor_scalar(QKp, QKp, EPS_ATTN, 1.0 - EPS_ATTN,
                                ALU.max, ALU.min)
                c.inc(v.tensor_scalar(QKp[:, 0:256], QKp[:, 0:256], SQ_DH,
                                      None, ALU.mult), se_aux)
                s.wait_ge(se_aux, c.val(se_aux))
                for hh in range(2):
                    kv = kTf[:, hh * N + ch * 128: hh * N + (ch + 1) * 128]
                    c.inc(s.activation(tA[:, hh * 128:(hh + 1) * 128], kv,
                                       AF.Abs), se_sq)
                s.wait_ge(se_sq, c.val(se_sq))
                c.inc(s.activation(tBp, tA[:], AF.Exp, scale=-1.0), se_sq)
                s.wait_ge(se_sq, c.val(se_sq))
                c.inc(s.activation(tBp, tBp, AF.Ln, bias=cst["one_p"]), se_sq)
                for hh in range(2):
                    kv = kTf[:, hh * N + ch * 128: hh * N + (ch + 1) * 128]
                    v.tensor_scalar(dTf[:, hh * 128:(hh + 1) * 128], kv,
                                    0.0, None, ALU.min)
                v.wait_ge(se_sq, c.val(se_sq))
                v.tensor_tensor(dTf[:], dTf[:], tBp, ALU.subtract)
                c.inc(v.tensor_scalar(dTf[:], dTf[:], -256.0, 0.0,
                                      ALU.max, ALU.min), se_aux)
                c.inc(v.scalar_tensor_tensor(
                    kbTp, dTf[:], 1.0 / GATE_NORM, QKp[:, 256:512],
                    ALU.mult, ALU.mult), se_aux)
                prep_aux = c.val(se_aux)

                qlT = [QKp[:, hh * 128:(hh + 1) * 128] for hh in range(2)]
                klT = [QKp[:, 256 + hh * 128: 256 + (hh + 1) * 128]
                       for hh in range(2)]
                kbTh = [kbTp[:, hh * 128:(hh + 1) * 128] for hh in range(2)]
                kbh = [kbp[:, hh * 128:(hh + 1) * 128] for hh in range(2)]
                vph = [vpp[:, hh * 128:(hh + 1) * 128] for hh in range(2)]
                klh = [kltp[:, hh * 128:(hh + 1) * 128] for hh in range(2)]
                Ssb = [S_sb[:, hh * 128:(hh + 1) * 128] for hh in range(2)]

                mm_h = [0, 0]
                ev_h = [0, 0]
                ysq_h = [0, 0]
                t.wait_ge(se_aux, prep_aux)
                for hh in range(2):
                    bk = banks[2 * hh]
                    if gch >= 1:
                        t.wait_ge(se_ev, chain_free[hh])
                        t.wait_ge(se_sq, z_free[hh])
                    t.matmul(bk[:, 0:128], klT[hh], kbTh[hh],
                             start=True, stop=True)
                    c.inc(t.matmul(bk[:, 128:256], kbTh[hh], klT[hh],
                                   start=True, stop=True), se_mm)
                    mm_h[hh] = c.val(se_mm)
                for hh in range(2):
                    bk = banks[2 * hh]
                    v.wait_ge(se_mm, mm_h[hh])
                    v.tensor_tensor(Au[hh][:], bk[:, 0:128], maub[:],
                                    ALU.mult)
                    v.tensor_tensor(Ybuf[hh][:], bk[:, 128:256], my0b[:],
                                    ALU.mult)
                    v.tensor_copy(chainB[hh][:, 0:128], id128b[:])
                    c.inc(v.tensor_copy(chainB[hh][:, 128:256], Au[hh][:]),
                          se_ev)
                    ev_h[hh] = c.val(se_ev)
                for lvl in range(7):
                    for hh in range(2):
                        pa = banks[2 * hh + (lvl + 1) % 2]
                        t.wait_ge(se_ev, ev_h[hh])
                        if ysq_h[hh]:
                            t.wait_ge(se_sq, ysq_h[hh])
                        if lvl < 6:
                            t.matmul(pa[:, 0:256], Ybuf[hh][:],
                                     chainB[hh][:, 0:256],
                                     start=True, stop=True)
                            c.inc(t.matmul(pa[:, 256:384],
                                           chainB[hh][:, 128:256],
                                           Ybuf[hh][:], start=True,
                                           stop=True), se_mm)
                        else:
                            c.inc(t.matmul(pa[:, 0:128], Ybuf[hh][:],
                                           chainB[hh][:, 0:128],
                                           start=True, stop=True), se_mm)
                        mm_h[hh] = c.val(se_mm)
                    for hh in range(2):
                        pa = banks[2 * hh + (lvl + 1) % 2]
                        v.wait_ge(se_mm, mm_h[hh])
                        if lvl < 6:
                            if OPT_CHAIN_ACT:
                                s.wait_ge(se_mm, mm_h[hh])
                                c.inc(s.activation(Ybuf[hh][:],
                                                   pa[:, 256:384],
                                                   AF.Copy), se_sq)
                                ysq_h[hh] = c.val(se_sq)
                            else:
                                c.inc(v.tensor_copy(Ybuf[hh][:],
                                                    pa[:, 256:384]), se_ev)
                            v.tensor_tensor(chainB[hh][:, 0:128],
                                            pa[:, 0:128],
                                            chainB[hh][:, 0:128], ALU.add)
                            c.inc(v.tensor_copy(chainB[hh][:, 128:256],
                                                pa[:, 128:256]), se_ev)
                        else:
                            c.inc(v.tensor_tensor(
                                Tt[hh][:], pa[:, 0:128],
                                chainB[hh][:, 0:128], ALU.add), se_ev)
                        ev_h[hh] = c.val(se_ev)

                sq_h = [0, 0]
                if ch > 0:
                    for hh in range(2):
                        t.wait_ge(se_ev, ev_h[hh])
                        t.wait_ge(se_ev, s_free[hh])
                        c.inc(t.matmul(banks[2 * hh][:, 0:128], kbh[hh],
                                       Tt[hh][:], start=True, stop=True),
                              se_mm)
                        mm_h[hh] = c.val(se_mm)
                    for hh in range(2):
                        s.wait_ge(se_mm, mm_h[hh])
                        c.inc(s.activation(nwT[hh][:],
                                           banks[2 * hh][:, 0:128],
                                           AF.Copy, scale=-1.0), se_sq)
                        sq_h[hh] = c.val(se_sq)
                for hh in range(2):
                    if ch > 0:
                        t.wait_ge(se_sq, sq_h[hh])
                    else:
                        t.wait_ge(se_ev, ev_h[hh])
                    bk1 = banks[2 * hh + 1]
                    t.matmul(bk1[:, 0:128], Tt[hh][:], vph[hh],
                             start=True, stop=(ch == 0))
                    if ch > 0:
                        c.inc(t.matmul(bk1[:, 0:128], nwT[hh][:], Ssb[hh],
                                       start=False, stop=True), se_mm)
                    else:
                        c.inc(t.nop(), se_mm)
                    mm_h[hh] = c.val(se_mm)
                for hh in range(2):
                    v.wait_ge(se_mm, mm_h[hh])
                    c.inc(v.tensor_copy(up_tm[hh][:],
                                        banks[2 * hh + 1][:, 0:128]), se_ev)
                    ev_h[hh] = c.val(se_ev)
                for hh in range(2):
                    c.inc(t.matmul(banks[2 * hh][:, 128:256], klT[hh],
                                   qlT[hh], start=True, stop=True), se_mm)
                    mm_h[hh] = c.val(se_mm)
                for hh in range(2):
                    v.wait_ge(se_mm, mm_h[hh])
                    c.inc(v.tensor_tensor(atT[hh][:],
                                          banks[2 * hh][:, 128:256],
                                          mincb[:], ALU.mult), se_ev)
                    ev_h[hh] = c.val(se_ev)
                for hh in range(2):
                    t.wait_ge(se_ev, ev_h[hh])
                    bk1 = banks[2 * hh + 1]
                    if ch > 0:
                        t.matmul(bk1[:, 128:256], Ssb[hh], qlT[hh],
                                 start=True, stop=False)
                    c.inc(t.matmul(bk1[:, 128:256], up_tm[hh][:], atT[hh][:],
                                   start=(ch == 0), stop=True), se_mm)
                    mm_h[hh] = c.val(se_mm)
                for hh in range(2):
                    s.wait_ge(se_mm, mm_h[hh])
                    c.inc(s.activation(
                        zT[:, (b * 2 + hh) * N + ch * 128:
                           (b * 2 + hh) * N + (ch + 1) * 128],
                        banks[2 * hh + 1][:, 128:256],
                        AF.Copy, scale=0.5), se_sq)
                    sq_h[hh] = c.val(se_sq)
                    z_free[hh] = c.val(se_sq)
                for hh in range(2):
                    t.wait_ge(se_ev, s_free[hh])
                    c.inc(t.matmul(S_banks[hh][:, 0:128], klh[hh],
                                   up_tm[hh][:], start=(ch == 0), stop=True),
                          se_mm)
                    mm_h[hh] = c.val(se_mm)
                for hh in range(2):
                    v.wait_ge(se_mm, mm_h[hh])
                    c.inc(v.tensor_copy(Ssb[hh], S_banks[hh][:, 0:128]),
                          se_ev)
                    s_free[hh] = c.val(se_ev)
                    chain_free[hh] = c.val(se_ev)
                mmsnap[gch] = c.val(se_mm)
                if DEBUG_JOIN_CHUNK == 1 or (DEBUG_JOIN_CHUNK == 2 and tt == 3):
                    join_all(c, sem)

        v.wait_ge(se_sq, c.val(se_sq))
        prev_ev = c.val(se_ev)
        prev_aux = c.val(se_aux)
        prev_sq = c.val(se_sq)
        for hh in range(2):
            pair = b * 2 + hh
            for qb in range(8):
                nkt = 4 * qb + 4
                qb_mm0 = c.val(se_mm)
                stm, esq, amm = [], [], []

                def emit_ST(kt):
                    pS = banks[kt % 2]
                    if kt >= 2:
                        sm_, vv_ = esq[kt - 2]
                        t.wait_ge(sm_, vv_)
                    c.inc(t.matmul(
                        pS[:, 0:512],
                        kTf[:, hh * N + kt * 128: hh * N + (kt + 1) * 128],
                        qTf[:, hh * N + qb * 512: hh * N + qb * 512 + 512],
                        start=True, stop=True), se_mm)
                    stm.append(c.val(se_mm))

                emit_ST(0)
                for kt in range(nkt):
                    if kt + 1 < nkt:
                        emit_ST(kt + 1)
                    ptv = pt[:, (kt % 2) * 512:(kt % 2) * 512 + 512]
                    s.wait_ge(se_mm, stm[kt])
                    s.wait_ge(se_mm, amm[kt - 2] if kt >= 2 else qb_mm0)
                    c.inc(s.activation(ptv, banks[kt % 2][:, 0:512], AF.Exp,
                                       scale=SQ_DH), se_sq)
                    if kt >= 4 * qb:
                        v.wait_ge(se_sq, c.val(se_sq))
                        v.wait_ge(se_mm, amm[kt - 2] if kt >= 2 else qb_mm0)
                        c.inc(v.tensor_tensor(
                            ptv, ptv,
                            mptb[:, (kt - 4 * qb) * 512:
                                 (kt - 4 * qb + 1) * 512], ALU.mult), se_aux)
                        t.wait_ge(se_aux, c.val(se_aux))
                        esq.append((se_aux, c.val(se_aux)))
                    else:
                        t.wait_ge(se_sq, c.val(se_sq))
                        esq.append((se_sq, c.val(se_sq)))
                    if kt == 0:
                        t.wait_ge(se_ev, prev_ev)
                        t.wait_ge(se_aux, prev_aux)
                        t.wait_ge(se_sq, prev_sq)
                    t.matmul(banks[2][:, 0:512],
                             vtm[:, (b * 32 + kt) * 256 + hh * 128:
                                 (b * 32 + kt) * 256 + (hh + 1) * 128],
                             ptv, start=(kt == 0), stop=(kt == nkt - 1))
                    c.inc(t.matmul(banks[3][0:1, 0:512], cst["onescol"], ptv,
                                   start=(kt == 0), stop=(kt == nkt - 1)),
                          se_mm)
                    amm.append(c.val(se_mm))
                if OPT_PREP_ACTRECIP:
                    s.wait_ge(se_mm, c.val(se_mm))
                    c.inc(s.activation(csrow[:], banks[3][0:1, 0:512], AF.Ln),
                          se_sq)
                    s.wait_ge(se_sq, c.val(se_sq))
                    c.inc(s.activation(csbf[0:1, 0:512], csrow[:], AF.Exp,
                                       scale=-1.0), se_sq)
                    t.wait_ge(se_sq, c.val(se_sq))
                else:
                    v.wait_ge(se_mm, c.val(se_mm))
                    c.inc(v.reciprocal(csrow[:], banks[3][0:1, 0:512]),
                          se_aux)
                    v.wait_ge(se_aux, c.val(se_aux))
                    c.inc(v.tensor_copy(csbf[0:1, 0:512], csrow[:]), se_aux)
                    t.wait_ge(se_aux, c.val(se_aux))
                t.wait_ge(se_ev, c.val(se_ev))
                c.inc(t.matmul(banks[4][:, 0:512], cst["onesrow_bf16"],
                               csbf[0:1, 0:512], start=True, stop=True), se_mm)
                v.wait_ge(se_mm, c.val(se_mm))
                c.inc(v.tensor_copy(rB[:], banks[4][:, 0:512]), se_ev)
                v.wait_ge(se_ev, c.val(se_ev))
                c.inc(v.tensor_tensor(tmpO[:], banks[2][:, 0:512], rB[:],
                                      ALU.mult), se_ev)
                zsl = zT[:, pair * N + qb * 512: pair * N + qb * 512 + 512]
                v.wait_ge(se_ev, c.val(se_ev))
                c.inc(v.scalar_tensor_tensor(
                    zsl, tmpO[:], 0.5, zsl, ALU.mult, ALU.add), se_ev)
                prev_ev = c.val(se_ev)
                prev_aux = c.val(se_aux)
                prev_sq = c.val(se_sq)
            if b == 1 and hh == 0:
                g.wait_ge(se_ev, c.val(se_ev))
                g.wait_ge(se_sq, c.val(se_sq))
                launch_a2a(c, 0, a2a_in, a2a_out, zT, sem)

    join_all(c, sem)
    launch_a2a(c, 1, a2a_in, a2a_out, zT, sem)
    pool = [xb, wvkb, wqb, wkb]
    wo_prefetched = False
    if "wo_dram" in ap:
        if "wo_pre_gate" in ap:
            g.wait_ge(ap["wo_pre_gate"][0], ap["wo_pre_gate"][1])
        wo_r = ap["wo_dram"].rearrange("hh j p n -> p (hh j) n")
        kc0 = 0
        for buf, sz in zip(pool, POOL_SIZES):
            nkc = sz // D
            c.inc(g.dma_start(
                out=buf[:, 0:nkc * D].rearrange("p (a n) -> p a n", a=nkc),
                in_=wo_r[:, kc0:kc0 + nkc, :]), se_wdma, 16)
            kc0 += nkc
        wo_prefetched = True
    g.wait_ge(se_cc, c.val(se_cc))
    inner.close()
    c._scopes.pop()
    return {"z0": a2a_out[0], "z1": a2a_out[1], "sem": sem,
            "pool": pool, "wo_prefetched": wo_prefetched}


def build_stage1_program():
    nc = bass.Bass("TRN2", target_bir_lowering=False)
    ap = declare_stage1_io(nc, external_z=True)
    with ExitStack() as es:
        c = Ctx(nc, es)
        es.enter_context(nc.Block())
        cst = setup_consts(c, nc)
        banks = [c.ps(f"s1pb{i}") for i in range(8)]
        r = emit_stage1(c, ap, cst, banks)
        sem = r["sem"]
        for hh in range(2):
            c.inc(c.g.dma_start(out=ap["z%d" % hh], in_=r["z%d" % hh]),
                  sem["dma"], 16)
        c.g.wait_ge(sem["dma"], c.val(sem["dma"]))
    return nc


def stage1_in_map(m):
    keys = ["xtt", "wq", "wk", "wvk", "id128", "mau", "my0", "minc", "mpt",
            "onescol", "onesrow_f32", "onesrow_bf16"]
    return {k: m[k] for k in keys}



def host_prep(inputs):
    x = np.ascontiguousarray(inputs["L0_attn_x"])
    wq = np.asarray(inputs["L0_attn_Wq"])
    wk = np.asarray(inputs["L0_attn_Wk"])
    wv = np.asarray(inputs["L0_attn_Wv"])
    wo = np.asarray(inputs["L0_attn_Wo"])
    w0 = np.asarray(inputs["L0_ffn_weight"])
    b0 = np.asarray(inputs["L0_ffn_bias"])
    w1 = np.asarray(inputs["L1_attn_W"])
    d1 = np.asarray(inputs["L1_attn_delta_W"])
    wg = np.asarray(inputs["L1_ffn_w_gate"])
    wu = np.asarray(inputs["L1_ffn_w_up"])
    wd = np.asarray(inputs["L1_ffn_w_down"])

    xtt = np.ascontiguousarray(
        x.reshape(B, 8, 512, 16, 128).transpose(0, 1, 4, 3, 2))
    wo_r = np.ascontiguousarray(wo.reshape(8, 2, 128, D).transpose(1, 0, 2, 3))
    w0t = np.ascontiguousarray(w0.T)
    b0r = np.ascontiguousarray(b0.reshape(16, 128).T)
    w1t = np.ascontiguousarray(w1.T)
    d1t = np.ascontiguousarray(d1.T)

    j = np.arange(8192)
    perm = (j % 2048) * 4 + j // 2048
    wgp = np.ascontiguousarray(
        wg[:, perm].reshape(16, 128, 64, 128).transpose(2, 0, 1, 3))
    wup = np.ascontiguousarray(
        wu[:, perm].reshape(16, 128, 64, 128).transpose(2, 0, 1, 3))
    wdp = np.ascontiguousarray(wd[perm, :].reshape(64, 128, D))
    wgup = np.ascontiguousarray(np.stack([wgp, wup], axis=1))

    id128 = np.eye(128).astype(ml_dtypes.bfloat16)
    r = np.arange(128)
    mau = (-(r[:, None] < r[None, :]).astype(np.float32)).astype(
        ml_dtypes.bfloat16)
    my0 = (-(r[:, None] > r[None, :]).astype(np.float32)).astype(
        ml_dtypes.bfloat16)
    minc = (r[:, None] <= r[None, :]).astype(np.float32).astype(
        ml_dtypes.bfloat16)
    mpt = np.zeros((4, 128, 512), np.float32)
    for i in range(4):
        for sgm in range(4):
            if sgm > i:
                mpt[i, :, sgm * 128:(sgm + 1) * 128] = 1.0
            elif sgm == i:
                mpt[i, :, sgm * 128:(sgm + 1) * 128] = (
                    r[:, None] <= r[None, :]).astype(np.float32)
    mpt = mpt.astype(ml_dtypes.bfloat16)
    onescol = np.ones((128, 1), ml_dtypes.bfloat16)
    onesrow_f32 = np.ones((1, 128), np.float32)
    onesrow_bf16 = np.ones((1, 128), ml_dtypes.bfloat16)

    fx = x.reshape(B * N, D)
    xt_full = np.ascontiguousarray(fx.T)

    in_maps = []
    for cix in range(NCORE):
        m = dict(
            xtt=xtt,
            xt_tok=np.ascontiguousarray(xt_full[:, cix * NTOK:(cix + 1) * NTOK]),
            wq=np.ascontiguousarray(wq[:, 2 * cix * 128:(2 * cix + 2) * 128]),
            wk=np.ascontiguousarray(wk[:, 2 * cix * 128:(2 * cix + 2) * 128]),
            wvk=np.ascontiguousarray(np.concatenate(
                [wv[:, 2 * cix * 128:(2 * cix + 2) * 128],
                 wk[:, 2 * cix * 128:(2 * cix + 2) * 128]], axis=1)),
            wo=wo_r, w0t=w0t, b0r=b0r, w1t=w1t, d1t=d1t,
            wgp=wgp, wup=wup, wgup=wgup, wdp=wdp,
            id128=id128, mau=mau, my0=my0, minc=minc, mpt=mpt,
            onescol=onescol, onesrow_f32=onesrow_f32,
            onesrow_bf16=onesrow_bf16,
        )
        in_maps.append(m)
    return in_maps



def setup_consts(c, nc):
    names = {}
    for nm, shape, dt in (("onescol", [128, 1], BF16),
                          ("onesrow_f32", [1, 128], F32),
                          ("onesrow_bf16", [1, 128], BF16)):
        names[nm] = nc.dram_tensor(nm, shape, dt, kind="ExternalInput").ap()
    onescol = c.sb("onescol_s", [128, 1], BF16)
    onesrow = c.sb("onesrow_s", [1, 128], F32)
    onesrowb = c.sb("onesrowb_s", [1, 128], BF16)
    cpool = c.sb("cpool", [128, 4], F32)
    cst = {"onescol": onescol[:], "onesrow_f32": onesrow[:],
           "onesrow_bf16": onesrowb[:],
           "eps_ln_1": cpool[0:1, 0:1], "one_p": cpool[:, 1:2]}
    c.v.memset(cpool[:, 0:1], EPS_LN)
    c.v.memset(cpool[:, 1:2], 1.0)
    sed = c.sem("cst_init")
    c.inc(c.y.dma_start(out=onescol[:], in_=names["onescol"]), sed, 16)
    c.inc(c.y.dma_start(out=onesrow[:], in_=names["onesrow_f32"]), sed, 16)
    c.inc(c.y.dma_start(out=onesrowb[:], in_=names["onesrow_bf16"]), sed, 16)
    for e in (c.t, c.v, c.s, c.g, c.y):
        e.wait_ge(sed, c.val(sed))
    return cst


def build_stage2_program(dbg_names=()):
    nc = bass.Bass("TRN2", target_bir_lowering=False)
    ap = declare_stage2_io(nc)
    z_src = {
        "z0": nc.dram_tensor("z0", [NTOK, NTOK], BF16, kind="ExternalInput").ap(),
        "z1": nc.dram_tensor("z1", [NTOK, NTOK], BF16, kind="ExternalInput").ap(),
    }
    dbg = {}
    for nm in dbg_names:
        dbg[nm] = nc.dram_tensor("dbg_" + nm, [128, DC * NTOK], F32,
                                 kind="ExternalOutput").ap()
    with ExitStack() as es:
        c = Ctx(nc, es)
        es.enter_context(nc.Block())
        cst = setup_consts(c, nc)
        banks = [c.ps(f"pb{i}") for i in range(8)]
        emit_stage2(c, ap, z_src, cst, banks, dbg=dbg)
    return nc


def build_fused_program(dbg_names=()):
    nc = bass.Bass("TRN2", target_bir_lowering=False)
    ap1 = declare_stage1_io(nc, external_z=False)
    ap2 = declare_stage2_io(nc)
    dbg = {}
    for nm in dbg_names:
        dbg[nm] = nc.dram_tensor("dbg_" + nm, [128, DC * NTOK], F32,
                                 kind="ExternalOutput").ap()
    with ExitStack() as es:
        c = Ctx(nc, es)
        es.enter_context(nc.Block())
        cst = setup_consts(c, nc)
        banks = [c.ps(f"pb{i}") for i in range(8)]
        pre, se_pre, pre_n, pieces = emit_prepass(c, ap2, nc)
        ap2["pre"] = pre
        ap1["prepass_pieces"] = pieces
        ap1["wo_dram"] = pre["wo"]
        ap1["wo_pre_gate"] = (se_pre, pre_n)
        r = emit_stage1(c, ap1, cst, banks, dbg=dbg)
        s1_sem = r["sem"]
        join_all(c, s1_sem)
        for e in (c.t, c.v, c.s, c.g, c.y):
            e.wait_ge(se_pre, pre_n)
        emit_stage2(c, ap2, {"z0": r["z0"], "z1": r["z1"]}, cst, banks,
                    dbg=dbg, pool=r["pool"], wo_prefetched=r["wo_prefetched"])
    return nc


def stage2_in_map(m, z0, z1):
    keys = ["xt_tok", "wo", "w0t", "b0r", "w1t", "d1t", "wgp", "wup", "wgup",
            "wdp", "onescol", "onesrow_f32", "onesrow_bf16"]
    d = {k: m[k] for k in keys}
    d["z0"] = z0
    d["z1"] = z1
    return d


_PROG_CACHE = {}


def _get_prog(name, builder):
    if name not in _PROG_CACHE:
        _PROG_CACHE[name] = builder()
    return _PROG_CACHE[name]


def fused_in_map(m):
    keys = ["xtt", "wq", "wk", "wvk", "id128", "mau", "my0", "minc", "mpt",
            "onescol", "onesrow_f32", "onesrow_bf16",
            "xt_tok", "wo", "w0t", "b0r", "w1t", "d1t", "wgp", "wup", "wgup",
            "wdp"]
    return {k: m[k] for k in keys}


def kernel(**inputs):
    _ = inputs["L1_attn_target_W"]
    in_maps = host_prep(inputs)
    ncf = _get_prog("fused", build_fused_program)
    res = run_bass_kernel_spmd(
        ncf, [fused_in_map(m) for m in in_maps], core_ids=list(range(NCORE)))
    out = np.empty((B * N, D), np.float32)
    for cix in range(NCORE):
        o = np.asarray(res.results[cix]["out"])
        out[cix * NTOK:(cix + 1) * NTOK, :] = o.T
    return out.reshape(B, N, D)


def kernel_unfused(**inputs):
    in_maps = host_prep(inputs)

    nc1 = _get_prog("s1", build_stage1_program)
    res1 = run_bass_kernel_spmd(
        nc1, [stage1_in_map(m) for m in in_maps],
        core_ids=list(range(NCORE)))

    nc2 = _get_prog("s2", lambda: build_stage2_program(()))
    maps2 = []
    for cix in range(NCORE):
        maps2.append(stage2_in_map(
            in_maps[cix],
            np.asarray(res1.results[cix]["z0"]),
            np.asarray(res1.results[cix]["z1"])))
    res2 = run_bass_kernel_spmd(nc2, maps2, core_ids=list(range(NCORE)))

    out = np.empty((B * N, D), np.float32)
    for cix in range(NCORE):
        o = np.asarray(res2.results[cix]["out"])
        out[cix * NTOK:(cix + 1) * NTOK, :] = o.T
    return out.reshape(B, N, D)
